# version 03
# baseline (speedup 1.0000x reference)
import math
from contextlib import ExitStack

import numpy as np

import concourse.bass as bass
import concourse.mybir as mybir
from concourse.bass_utils import run_bass_kernel_spmd

F32 = mybir.dt.float32
BF16 = mybir.dt.bfloat16
AF = mybir.ActivationFunctionType
ALU = mybir.AluOpType

MAGIC = float(1.5 * 2 ** 23)
C = 1024
CH = 8
NCORES = 8
TOK = 8192
LN_EPS = 1e-5

ENGINES = ("sync", "gpsimd", "tensor", "vector", "scalar")


class Prog:

    def __init__(self, name, detect_races=True):
        self.nc = bass.Bass("TRN2", target_bir_lowering=False, name=name,
                            detect_race_conditions=detect_races)
        self.stack = ExitStack()
        self.stages = []
        self.sems = {}
        self.cnt = {}
        self._consts = {}
        self._ctile = None
        self._tiles = {}
        self.guard = {}
        self.ones1f = self.sbuf("ones1f", [1, 128], F32)
        self.ones1b = self.sbuf("ones1b", [1, 128], BF16)
        self.onescf = self.sbuf("onescf", [128, 1], F32)
        self.onescb = self.sbuf("onescb", [128, 1], BF16)
        self._ctile = self.sbuf("constpool", [128, 16], F32)

    def sbuf(self, name, shape, dt):
        cm = self.nc.sbuf_tensor(name, shape, dt)
        self._tiles[name] = cm
        return cm.__enter__()

    def sfree(self, name):
        self._tiles.pop(name).__exit__(None, None, None)

    def psum(self, name):
        cm = self.nc.psum_tensor(name, [128, 512], F32)
        self._tiles[name] = cm
        return cm.__enter__()

    def const(self, v):
        v = float(v)
        if v not in self._consts:
            i = len(self._consts)
            assert i < 16
            self._consts[v] = i
            self.nc.const_aps.register(F32, v, self._ctile[:, i:i + 1])
        return self._ctile[:, self._consts[v]:self._consts[v] + 1]

    def const_p(self, v, np_):
        full = self.const(v)
        return self._ctile[0:np_, self._consts[float(v)]:self._consts[float(v)] + 1]

    def sem(self, name):
        if name not in self.sems:
            self.sems[name] = self.stack.enter_context(self.nc.semaphore(name))
            self.cnt[name] = 0
        return self.sems[name]

    def inc(self, name, n=1):
        self.sem(name)
        self.cnt[name] += n
        return self.cnt[name]

    def val(self, name):
        return self.cnt.get(name, 0)

    def stage(self, name):
        self.stages.append((name, {e: [] for e in ENGINES}))

    def on(self, eng, fn):
        self.stages[-1][1][eng].append(fn)

    def wait(self, eng, sem_name, v=None):
        v = self.cnt[sem_name] if v is None else v
        if v <= 0:
            return
        s = self.sem(sem_name)
        self.on(eng, lambda E, s=s, v=v: E.wait_ge(s, v))

    def set_guard(self, buf, sem_name, v=None):
        self.guard[buf] = (sem_name, self.cnt[sem_name] if v is None else v)

    def wait_buf(self, eng, buf):
        g = self.guard.pop(buf, None)
        if g is not None:
            self.wait(eng, g[0], g[1])

    def dma(self, qeng, dst, src, sem_name):
        tgt = self.inc(sem_name, 16)
        s = self.sems[sem_name]
        self.on(qeng, lambda E, d=dst, i=src, s=s: E.dma_start(d, i).then_inc(s, 16))
        return tgt

    def mm(self, psum_ap, pairs, sem_name=None, inc=1):
        s = self.sem(sem_name) if sem_name else None
        tgt = self.inc(sem_name, inc) if sem_name else None

        def run(t, psum_ap=psum_ap, pairs=tuple(pairs), s=s, inc=inc):
            ins = None
            n = len(pairs)
            for i, (l, r) in enumerate(pairs):
                ins = t.matmul(psum_ap, l, r, start=(i == 0), stop=(i == n - 1))
            if s is not None:
                ins.then_inc(s, inc)
        self.on("tensor", run)
        return tgt

    def op(self, eng, fname, *args, sem_name=None, inc=1, **kw):
        s = self.sem(sem_name) if sem_name else None
        if sem_name:
            self.inc(sem_name, inc)

        def run(E, fname=fname, args=args, kw=kw, s=s, inc=inc):
            ins = getattr(E, fname)(*args, **kw)
            if s is not None:
                ins.then_inc(s, inc)
        self.on(eng, run)
        return self.val(sem_name) if sem_name else None

    def init_stage(self):
        self.stage("init")

        def run(v, prog=self):
            v.memset(prog.ones1f[:, :], 1.0)
            v.memset(prog.ones1b[:, :], 1.0)
            v.memset(prog.onescf[:, :], 1.0)
            v.memset(prog.onescb[:, :], 1.0)
            if prog._ctile is not None:
                for val, i in prog._consts.items():
                    v.memset(prog._ctile[:, i:i + 1], val)
        self.on("vector", run)

    def finalize(self):
        nc = self.nc
        for name, ops in self.stages:
            with nc.Block(name) as blk:
                for e in ENGINES:
                    if ops[e]:
                        def runner(E, ops=tuple(ops[e])):
                            for f in ops:
                                f(E)
                        getattr(blk, e)(runner)
        self.stack.close()
        return nc



def emit_ln_tile(p, ps, x_ap, y_tile, W, gain_col=None, beta_col=None):
    xb, tscr = ps["xb"], ps["tscr"]
    rows1, rows2 = ps["rows1"], ps["rows2"]
    r_m = rows1[0:1, 0:W]
    r_t = rows1[32:33, 0:W]
    r_v = rows1[64:65, 0:W]
    r_r = rows2[0:1, 0:W]
    r_b = rows2[0:1, W:2 * W]
    psS, psQ, psRB, psBB = ps["psS"], ps["psQ"], ps["psRB"], ps["psBB"]
    gk = ps.get("gk", {})
    kS = gk.get("S", "ln_psS")
    kQ = gk.get("Q", "ln_psQ")
    kRB = gk.get("RB", "ln_psRB")
    kBB = gk.get("BB", "ln_psBB")
    kxb = gk.get("xb", "ln_xb")
    ktsc = gk.get("tscr", "ln_tscr")

    p.wait_buf("vector", kxb)
    t0 = p.op("vector", "tensor_copy", xb[:, :, :], x_ap, sem_name="lndve")
    p.wait("tensor", "lndve", t0)
    p.wait_buf("tensor", kS)
    tS = p.mm(psS[0:1, 0:W], [(p.onescb[:, :], xb[:, ch, :]) for ch in range(CH)],
              sem_name="lnmm")
    p.wait("scalar", "lnmm", tS)
    t1 = p.op("scalar", "activation", xb[:, :, :], xb[:, :, :], AF.Square,
              sem_name="lnact")
    p.wait("tensor", "lnact", t1)
    p.wait_buf("tensor", kQ)
    tmm = p.mm(psQ[0:1, 0:W], [(p.onescb[:, :], xb[:, ch, :]) for ch in range(CH)],
               sem_name="lnmm")
    p.set_guard(kxb, "lnmm", tmm)
    p.wait("vector", "lnmm", tmm)
    p.op("vector", "tensor_scalar", r_m, psS[0:1, 0:W], 1.0 / C, None, ALU.mult)
    p.op("vector", "tensor_tensor", r_t, r_m, r_m, ALU.mult)
    t2 = p.op("vector", "scalar_tensor_tensor", r_v, psQ[0:1, 0:W], 1.0 / C, r_t,
              ALU.mult, ALU.subtract, sem_name="lndve")
    p.set_guard(kS, "lndve", t2)
    p.set_guard(kQ, "lndve", t2)
    p.wait("scalar", "lndve", t2)
    p.op("scalar", "activation", r_t, r_v, AF.Ln, bias=p.const_p(LN_EPS, 1))
    t3 = p.op("scalar", "activation", r_r, r_t, AF.Exp, scale=-0.5,
              sem_name="lnact")
    p.wait("vector", "lnact", t3)
    t4 = p.op("vector", "scalar_tensor_tensor", r_b, r_m, -1.0, r_r,
              ALU.mult, ALU.mult, sem_name="lndve")
    p.wait("tensor", "lndve", t4)
    p.wait_buf("tensor", kRB)
    p.mm(psRB[:, 0:W], [(p.ones1f[:, :], r_r)])
    p.wait_buf("tensor", kBB)
    tbb = p.mm(psBB[:, 0:W], [(p.ones1f[:, :], r_b)], sem_name="lnmm")
    p.wait("vector", "lnmm", tbb)
    p.wait_buf("vector", ktsc)
    for ch in range(CH):
        p.op("vector", "tensor_tensor", tscr[:, ch, :], x_ap[:, ch, :],
             psRB[:, 0:W], ALU.mult)
    tail = None
    for ch in range(CH):
        if gain_col is None:
            tail = p.op("vector", "tensor_tensor", y_tile[:, ch, :],
                        tscr[:, ch, :], psBB[:, 0:W], ALU.add,
                        sem_name="lndve" if ch == CH - 1 else None)
        else:
            p.op("vector", "tensor_tensor", tscr[:, ch, :], tscr[:, ch, :],
                 psBB[:, 0:W], ALU.add)
            tail = p.op("vector", "tensor_scalar", y_tile[:, ch, :],
                        tscr[:, ch, :], gain_col[ch], beta_col[ch],
                        ALU.mult, ALU.add,
                        sem_name="lndve" if ch == CH - 1 else None)
    p.set_guard(kRB, "lndve", tail)
    p.set_guard(kBB, "lndve", tail)
    return tail


def wt_ap(w, ch, dc):
    return w[:, ch, dc * 128:(dc + 1) * 128]




def make_ln_scratch(p, W, pfx=""):
    return dict(
        xb=p.sbuf(pfx + "lnxb", [128, CH, W], BF16),
        tscr=p.sbuf(pfx + "lntscr", [128, CH, W], F32),
        rows1=p.sbuf(pfx + "lnrows1", [128, W], F32),
        rows2=p.sbuf(pfx + "lnrows2", [128, 2 * W], F32),
        psS=p.psum(pfx + "psS"), psQ=p.psum(pfx + "psQ"),
        psRB=p.psum(pfx + "psRB"), psBB=p.psum(pfx + "psBB"),
    )



NB = 32
PP = 256


def build_f1():
    p = Prog("pf1")
    nc = p.nc
    x_in = nc.dram_tensor("x", [C, TOK], F32, kind="ExternalInput").ap()
    x_out = nc.dram_tensor("xo", [C, TOK], F32, kind="ExternalOutput").ap()
    wnames = ["wf", "uf", "wi", "ui", "wc", "uc"]
    w_in = {n: nc.dram_tensor(n, [C, C], F32, kind="ExternalInput").ap()
            for n in wnames + ["wo", "uo"]}
    b_in = {n: nc.dram_tensor(n, [128, CH], F32, kind="ExternalInput").ap()
            for n in ["bf", "bi", "bc", "bo"]}
    c0_in = nc.dram_tensor("c0", [128, CH * PP], F32, kind="ExternalInput").ap()
    y_dram = nc.dram_tensor("ydram", [C, TOK], BF16, kind="Internal").ap()

    xr = x_in.rearrange("(ch p) t -> p ch t", p=128)
    yr = y_dram.rearrange("(ch p) t -> p ch t", p=128)
    xor_ = x_out.rearrange("(ch p) t -> p ch t", p=128)

    B = {n: p.sbuf("b_" + n, [128, CH], F32) for n in ["bf", "bi", "bc", "bo"]}
    cb = p.sbuf("cb", [128, CH, PP], BF16)
    W = {n: p.sbuf("w_" + n, [128, CH, C], BF16) for n in wnames}

    p.init_stage()
    for n in wnames:
        p.dma("gpsimd", W[n][:, :, :],
              w_in[n].rearrange("(ch p) d -> p ch d", p=128), "wld")
    for n in ["bf", "bi", "bc", "bo"]:
        p.dma("sync", B[n][:, :], b_in[n][:, :], "wldb")

    p.stage("ln")
    TS = 512
    NT = TOK // TS
    xt = [p.sbuf(f"xt{i}", [128, CH, TS], F32) for i in range(2)]
    yt = [p.sbuf(f"yt{i}", [128, CH, TS], BF16) for i in range(2)]
    ps = make_ln_scratch(p, TS)
    for k in range(NT):
        bi = k % 2
        p.wait_buf("sync", f"f1xt{bi}")
        tgt = p.dma("sync", xt[bi][:, :, :], xr[:, :, k * TS:(k + 1) * TS], "xin")
        p.wait("vector", "xin", tgt)
        p.wait_buf("vector", f"f1yt{bi}")
        tail = emit_ln_tile(p, ps, xt[bi][:, :, :], yt[bi][:, :, :], TS)
        p.set_guard(f"f1xt{bi}", "lndve", tail)
        p.wait("sync", "lndve", tail)
        st = p.dma("sync", yr[:, :, k * TS:(k + 1) * TS], yt[bi][:, :, :], "yst")
        p.set_guard(f"f1yt{bi}", "yst", st)

    for n in ["lnrows2", "lnrows1", "lntscr", "lnxb", "yt1", "yt0", "xt1", "xt0",
              "psBB", "psRB", "psQ", "psS"]:
        p.sfree(n)
    p.stage("rec")
    p.wait("sync", "yst", p.val("yst"))
    cbuf = [p.sbuf(f"cst{i}", [128, CH, PP], F32) for i in range(2)]
    ystep = [p.sbuf(f"ys{i}", [128, CH, PP], BF16) for i in range(2)]
    fs = p.sbuf("fs", [128, CH, PP], F32)
    is_ = p.sbuf("is", [128, CH, PP], F32)
    gs = p.sbuf("gs", [128, CH, PP], F32)
    t1 = p.sbuf("t1", [128, CH, PP], F32)
    pg = [p.psum(f"pg{i}") for i in range(6)]

    p.dma("sync", cbuf[0][:, :, :].rearrange("p ch t -> p (ch t)"), c0_in[:, :],
          "cin")
    p.wait("vector", "cin", 16)
    tc = p.op("vector", "tensor_copy", cb[:, :, :], cbuf[0][:, :, :],
              sem_name="cbr")

    gates = [("bf", "wf", "uf", fs, AF.Sigmoid),
             ("bi", "wi", "ui", is_, AF.Sigmoid),
             ("bc", "wc", "uc", gs, AF.Tanh)]
    p.wait("tensor", "wld", p.val("wld"))
    p.wait("tensor", "wldb", p.val("wldb"))
    for b in range(NB):
        cold = cbuf[b % 2]
        cnew = cbuf[(b + 1) % 2]
        ybuf = ystep[b % 2]
        p.wait_buf("sync", f"f1ys{b % 2}")
        tgt = p.dma("sync", ybuf[:, :, :], yr[:, :, b * PP:(b + 1) * PP], "yld")
        p.wait("tensor", "yld", tgt)
        p.wait("tensor", "cbr", p.val("cbr"))
        lastmm = None
        for dc in range(CH):
            for gi, (bn, wn, un, dst, act) in enumerate(gates):
                bank = pg[(dc % 2) * 3 + gi]
                p.wait_buf("tensor", f"pgB{(dc % 2) * 3 + gi}")
                pairs = [(wt_ap(W[wn], ch, dc), ybuf[:, ch, :]) for ch in range(CH)]
                pairs += [(wt_ap(W[un], ch, dc), cb[:, ch, :]) for ch in range(CH)]
                tmm = p.mm(bank[:, 0:PP], pairs, sem_name="gmm")
                lastmm = tmm
                p.wait("scalar", "gmm", tmm)
                tact = p.op("scalar", "activation", dst[:, dc, :], bank[:, 0:PP],
                            act, bias=B[bn][:, dc:dc + 1], sem_name="gact")
                p.set_guard(f"pgB{(dc % 2) * 3 + gi}", "gact", tact)
        p.set_guard(f"f1ys{b % 2}", "gmm", lastmm)
        p.wait("vector", "gact", p.val("gact"))
        p.op("vector", "tensor_tensor", t1[:, :, :], is_[:, :, :], gs[:, :, :],
             ALU.mult)
        p.op("vector", "tensor_tensor", cnew[:, :, :], fs[:, :, :],
             cold[:, :, :], ALU.mult)
        p.op("vector", "tensor_tensor", cnew[:, :, :], cnew[:, :, :],
             t1[:, :, :], ALU.add)
        p.op("vector", "tensor_copy", cb[:, :, :], cnew[:, :, :], sem_name="cbr")

    for n in (["t1", "gs", "is", "fs", "ys1", "ys0", "cst1", "cst0"]
              + ["w_" + n for n in reversed(wnames)]
              + [f"pg{i}" for i in reversed(range(6))]):
        p.sfree(n)
    p.stage("h")
    Wo = p.sbuf("w_wo", [128, CH, C], BF16)
    Uo = p.sbuf("w_uo", [128, CH, C], BF16)
    p.dma("gpsimd", Wo[:, :, :],
          w_in["wo"].rearrange("(ch p) d -> p ch d", p=128), "wld2")
    p.dma("gpsimd", Uo[:, :, :],
          w_in["uo"].rearrange("(ch p) d -> p ch d", p=128), "wld2")
    cu = p.sbuf("cu", [128, CH, PP], F32)
    pcu = [p.psum(f"pcu{i}") for i in range(2)]
    p.wait("tensor", "wld2", p.val("wld2"))
    p.wait("tensor", "cbr", p.val("cbr"))
    for dc in range(CH):
        p.wait_buf("tensor", f"pcuB{dc % 2}")
        tmm = p.mm(pcu[dc % 2][:, 0:PP],
                   [(wt_ap(Uo, ch, dc), cb[:, ch, :]) for ch in range(CH)],
                   sem_name="cumm")
        p.wait("vector", "cumm", tmm)
        tdv = p.op("vector", "tensor_copy", cu[:, dc, :], pcu[dc % 2][:, 0:PP],
                   sem_name="cudve")
        p.set_guard(f"pcuB{dc % 2}", "cudve", tdv)

    hx = [p.sbuf(f"hx{i}", [128, CH, TS], F32) for i in range(2)]
    hy = [p.sbuf(f"hy{i}", [128, CH, TS], BF16) for i in range(2)]
    hp = p.sbuf("hp", [128, CH, TS], F32)
    hh = p.sbuf("hh", [128, CH, TS], F32)
    pho = [p.psum(f"pho{i}") for i in range(2)]
    cud = p.val("cudve")
    for k in range(NT):
        bi = k % 2
        p.wait_buf("sync", f"hy{bi}")
        ty = p.dma("sync", hy[bi][:, :, :], yr[:, :, k * TS:(k + 1) * TS], "hyld")
        p.wait_buf("gpsimd", f"hx{bi}")
        tx = p.dma("gpsimd", hx[bi][:, :, :], xr[:, :, k * TS:(k + 1) * TS],
                   "hxld")
        p.wait("tensor", "hyld", ty)
        lastmm = None
        for dc in range(CH):
            bank = pho[dc % 2]
            p.wait_buf("tensor", f"phoB{dc % 2}")
            tmm = p.mm(bank[:, 0:TS],
                       [(wt_ap(Wo, ch, dc), hy[bi][:, ch, :]) for ch in range(CH)],
                       sem_name="hmm")
            lastmm = tmm
            p.wait("vector", "hmm", tmm)
            if k == 0 and dc == 0:
                p.wait("vector", "cudve", cud)
            p.wait_buf("vector", f"hp{dc}")
            p.op("vector", "tensor_tensor", hp[:, dc, 0:PP], bank[:, 0:PP],
                 cu[:, dc, :], ALU.add)
            tdv = p.op("vector", "tensor_tensor", hp[:, dc, PP:TS],
                       bank[:, PP:TS], cu[:, dc, :], ALU.add, sem_name="hdve")
            p.set_guard(f"phoB{dc % 2}", "hdve", tdv)
            p.wait("scalar", "hdve", tdv)
            p.wait_buf("scalar", f"hh{dc}")
            tact = p.op("scalar", "activation", hh[:, dc, :], hp[:, dc, :],
                        AF.Tanh, bias=B["bo"][:, dc:dc + 1], sem_name="hact")
            p.set_guard(f"hp{dc}", "hact", tact)
        p.set_guard(f"hy{bi}", "hmm", lastmm)
        p.wait("vector", "hact", p.val("hact"))
        p.wait("vector", "hxld", tx)
        tdv = None
        for ch in range(CH):
            tdv = p.op("vector", "scalar_tensor_tensor", hx[bi][:, ch, :],
                       hh[:, ch, :], 0.5, hx[bi][:, ch, :], ALU.mult, ALU.add,
                       sem_name="hodve" if ch == CH - 1 else None)
        for dc in range(CH):
            p.set_guard(f"hh{dc}", "hodve", tdv)
        p.wait("sync", "hodve", tdv)
        st = p.dma("sync", xor_[:, :, k * TS:(k + 1) * TS], hx[bi][:, :, :],
                   "xost")
        p.set_guard(f"hx{bi}", "xost", st)
    p.wait("sync", "xost", p.val("xost"))
    return p.finalize()



SL = 2048


def emit_f0(p, pfx, src_r, dst_r, w_in, mur, omr, muk, omk):
    TS = 512
    NT = TOK // TS
    W = {n: p.sbuf(pfx + "w" + n, [128, CH, C], BF16) for n in ["r", "k", "v"]}
    for n in ["r", "k", "v"]:
        p.dma("gpsimd", W[n][:, :, :],
              w_in["f0" + n].rearrange("(ch p) d -> p ch d", p=128), pfx + "wl")
    xt = [p.sbuf(pfx + f"xt{i}", [128, CH, TS], F32) for i in range(2)]
    yt = [p.sbuf(pfx + f"yt{i}", [128, CH, TS + 1], BF16) for i in range(2)]
    ps = make_ln_scratch(p, TS, pfx)
    xr = p.sbuf(pfx + "xr", [128, CH, TS], BF16)
    xk = p.sbuf(pfx + "xk", [128, CH, TS], BF16)
    rr = p.sbuf(pfx + "rr", [128, CH, TS], F32)
    ka = p.sbuf(pfx + "ka", [128, CH, TS], BF16)
    kr = p.sbuf(pfx + "kr", [128, TS], BF16)
    ot = p.sbuf(pfx + "ot", [128, CH, TS], F32)
    pp = [p.psum(pfx + f"pp{i}") for i in range(2)]
    smm, sdve, sact = pfx + "mm", pfx + "dve", pfx + "act"
    p.wait("tensor", pfx + "wl", p.val(pfx + "wl"))
    for k in range(NT):
        bi = k % 2
        yb, ybp = yt[bi], yt[1 - bi]
        p.wait_buf("sync", pfx + f"xt{bi}")
        tgt = p.dma("sync", xt[bi][:, :, :], src_r[:, :, k * TS:(k + 1) * TS],
                    pfx + "xin")
        p.wait("vector", pfx + "xin", tgt)
        p.wait_buf("vector", pfx + f"yt{bi}")
        if k % 4 == 0:
            p.op("vector", "memset", yb[:, :, 0:1], 0.0)
        else:
            p.op("vector", "tensor_copy", yb[:, :, 0:1], ybp[:, :, TS:TS + 1])
        tail = emit_ln_tile(p, ps, xt[bi][:, :, :], yb[:, :, 1:TS + 1], TS)
        for ch in range(CH):
            p.op("vector", "tensor_scalar", xr[:, ch, :], yb[:, ch, 0:TS],
                 omr[:, ch:ch + 1], None, ALU.mult)
            p.op("vector", "scalar_tensor_tensor", xr[:, ch, :],
                 yb[:, ch, 1:TS + 1], mur[:, ch:ch + 1], xr[:, ch, :],
                 ALU.mult, ALU.add)
            p.op("vector", "tensor_scalar", xk[:, ch, :], yb[:, ch, 0:TS],
                 omk[:, ch:ch + 1], None, ALU.mult)
            t5 = p.op("vector", "scalar_tensor_tensor", xk[:, ch, :],
                      yb[:, ch, 1:TS + 1], muk[:, ch:ch + 1], xk[:, ch, :],
                      ALU.mult, ALU.add,
                      sem_name=sdve if ch == CH - 1 else None)
        p.set_guard(pfx + f"yt{bi}", sdve, t5)
        p.set_guard(pfx + f"xt{bi}_ln", sdve, t5)
        p.wait("tensor", sdve, t5)
        for dc in range(CH):
            bank = pp[dc % 2]
            p.wait_buf("tensor", pfx + f"ppB{dc % 2}")
            tmm = p.mm(bank[:, 0:TS],
                       [(wt_ap(W["r"], ch, dc), xr[:, ch, :]) for ch in range(CH)],
                       sem_name=smm)
            p.wait("scalar", smm, tmm)
            ta = p.op("scalar", "activation", rr[:, dc, :], bank[:, 0:TS],
                      AF.Sigmoid, sem_name=sact)
            p.set_guard(pfx + f"ppB{dc % 2}", sact, ta)
        for dc in range(CH):
            bank = pp[dc % 2]
            p.wait_buf("tensor", pfx + f"ppB{dc % 2}")
            tmm = p.mm(bank[:, 0:TS],
                       [(wt_ap(W["k"], ch, dc), xk[:, ch, :]) for ch in range(CH)],
                       sem_name=smm)
            p.wait("vector", smm, tmm)
            p.wait_buf("vector", pfx + "kr")
            td = p.op("vector", "tensor_scalar", kr[:, :], bank[:, 0:TS], 0.0,
                      None, ALU.max, sem_name=sdve)
            p.wait("scalar", sdve, td)
            ta = p.op("scalar", "activation", ka[:, dc, :], kr[:, :], AF.Square,
                      sem_name=sact)
            p.set_guard(pfx + "kr", sact, ta)
            p.set_guard(pfx + f"ppB{dc % 2}", sact, ta)
        p.wait("tensor", sact, p.val(sact))
        for dc in range(CH):
            bank = pp[dc % 2]
            p.wait_buf("tensor", pfx + f"ppB{dc % 2}")
            tmm = p.mm(bank[:, 0:TS],
                       [(wt_ap(W["v"], ch, dc), ka[:, ch, :]) for ch in range(CH)],
                       sem_name=smm)
            p.wait("vector", smm, tmm)
            p.op("vector", "scalar_tensor_tensor", ot[:, dc, :], bank[:, 0:TS],
                 0.5, rr[:, dc, :], ALU.mult, ALU.mult)
            td = p.op("vector", "tensor_tensor", ot[:, dc, :], ot[:, dc, :],
                      xt[bi][:, dc, :], ALU.add, sem_name=sdve)
            p.set_guard(pfx + f"ppB{dc % 2}", sdve, td)
        p.set_guard(pfx + f"xt{bi}", sdve, td)
        p.wait("sync", sdve, td)
        st = p.dma("sync", dst_r[:, :, k * TS:(k + 1) * TS], ot[:, :, :],
                   pfx + "xo")
        p.set_guard(pfx + "ot", pfx + "xo", st)
        p.wait_buf("vector", pfx + "ot")
    p.wait("sync", pfx + "xo", p.val(pfx + "xo"))
    for n in [pfx + "pp1", pfx + "pp0"]:
        p.sfree(n)
    for n in ["ot", "kr", "ka", "rr", "xk", "xr"]:
        p.sfree(pfx + n)
    for n in ["psBB", "psRB", "psQ", "psS", "lnrows2", "lnrows1", "lntscr", "lnxb"]:
        p.sfree(pfx + n)
    for n in ["yt1", "yt0", "xt1", "xt0", "wv", "wk", "wr"]:
        p.sfree(pfx + n)


def emit_a0(p, src_r, dst_r, w_in, qt_dram, at_dram, upto=9, dbg=None):
    TS = 512
    pfx = "a"
    W = {n: p.sbuf(pfx + "W" + n, [128, CH, C], BF16) for n in ["q", "k", "v"]}
    bq = p.sbuf(pfx + "bq", [128, CH], F32)
    bk = p.sbuf(pfx + "bk", [128, CH], F32)
    bo = p.sbuf(pfx + "bo", [128, CH], F32)
    bvr = p.sbuf(pfx + "bvr", [1, C], BF16)
    p.dma("sync", bq[:, :], w_in["a0bq"][:, :], pfx + "wl")
    p.dma("sync", bk[:, :], w_in["a0bk"][:, :], pfx + "wl")
    p.dma("sync", bo[:, :], w_in["a0bo"][:, :], pfx + "wl")
    p.dma("gpsimd", bvr[:, :], w_in["a0bv"][:, :], pfx + "wlg")

    KT = p.sbuf(pfx + "KT", [128, CH, SL], BF16)
    Vs = p.sbuf(pfx + "Vs", [128, 16, C], BF16)
    xt = p.sbuf(pfx + "xt", [128, CH, TS], F32)
    ps = make_ln_scratch(p, TS, pfx)
    yt = p.sbuf(pfx + "yt", [128, CH, TS], BF16)
    rows = p.sbuf(pfx + "rows", [128, SL], F32)
    knc = p.sbuf(pfx + "knc", [128, 16], F32)
    nmc = p.sbuf(pfx + "nmc", [128, 1], F32)
    rqb = p.sbuf(pfx + "rqb", [128, TS], F32)
    rsb = rqb
    es = p.sbuf(pfx + "es", [128, 16, TS], BF16)
    qst = es[:, 0:CH, :]
    sc1d = p.sbuf(pfx + "sc1", [128, 1, TS], F32)
    sc2d = p.sbuf(pfx + "sc2", [128, 1, TS], F32)
    sc1 = sc1d[:, 0, :]
    sc2 = sc2d[:, 0, :]
    mrow = p.sbuf(pfx + "mrow", [1, 8], F32)
    at = p.sbuf(pfx + "at", [128, CH, TS], BF16)
    pj = [p.psum(pfx + f"pj{i}") for i in range(2)]
    prow = p.psum(pfx + "prow")
    pcol = p.psum(pfx + "pcol")
    smm, sdve, sact = pfx + "mm", pfx + "dve", pfx + "act"
    qr = qt_dram.rearrange("(ch p) t -> p ch t", p=128)
    ar = at_dram.rearrange("(ch p) t -> p ch t", p=128)
    ot = ps["tscr"]
    Qq = yt
    sqs = at
    rQ = rows[0:1, :]
    rK = rows[32:33, :]
    rr_ = sc1[0:1, 0:TS]
    mx = mrow[0:1, 0:8]

    p.wait("tensor", pfx + "wl", p.val(pfx + "wl"))
    for b in (range(1) if (dbg and dbg != "y") else range(4)):
        t0 = b * SL
        for n, key in [("q", "a0q"), ("k", "a0k"), ("v", "a0v")]:
            p.wait_buf("gpsimd", pfx + "W" + n)
            p.dma("gpsimd", W[n][:, :, :],
                  w_in[key].rearrange("(ch p) d -> p ch d", p=128), pfx + "wlg")
        twl = p.val(pfx + "wlg")
        p.wait("tensor", pfx + "wlg", twl)
        for tb in range(4):
            p.wait_buf("sync", pfx + "xt")
            tgt = p.dma("sync", xt[:, :, :],
                        src_r[:, :, t0 + tb * TS:t0 + (tb + 1) * TS], pfx + "xin")
            p.wait("vector", pfx + "xin", tgt)
            p.wait_buf("vector", pfx + "yt")
            ytail = emit_ln_tile(p, ps, xt[:, :, :], yt[:, :, :], TS)
            p.set_guard(pfx + "xt", "lndve", ytail)
            p.wait("tensor", "lndve", ytail)
            if dbg == "y":
                p.wait("sync", "lndve", ytail)
                p.wait_buf("sync", pfx + "ydbg")
                tq = p.dma("sync", qr[:, :, t0 + tb * TS:t0 + (tb + 1) * TS],
                           yt[:, :, :], pfx + "qst")
                p.set_guard(pfx + "ydbg", pfx + "qst", tq)
                p.set_guard(pfx + "yt", pfx + "qst", tq)
                continue
            p.wait_buf("tensor", pfx + "qst_dma")
            p.wait_buf("scalar", pfx + "at_sq")
            for dc in range(CH):
                bank = pj[dc % 2]
                p.wait_buf("tensor", pfx + f"pjB{dc % 2}")
                tmm = p.mm(bank[:, 0:TS],
                           [(wt_ap(W["q"], ch, dc), yt[:, ch, :])
                            for ch in range(CH)], sem_name=smm)
                if dc == CH - 1:
                    p.set_guard(pfx + "Wq_inuse", smm, tmm)
                p.wait("vector", smm, tmm)
                td = p.op("vector", "tensor_scalar", qst[:, dc, :],
                          bank[:, 0:TS], bq[:, dc:dc + 1], None, ALU.add,
                          sem_name=sdve)
                p.set_guard(pfx + f"pjB{dc % 2}", sdve, td)
                p.wait("scalar", sdve, td)
                ta = p.op("scalar", "activation", sqs[:, dc, :], qst[:, dc, :],
                          AF.Square, sem_name=sact)
            p.wait("tensor", sact, p.val(sact))
            p.wait_buf("tensor", pfx + "prow")
            tmm = p.mm(prow[0:1, 0:TS],
                       [(p.onescb[:, :], sqs[:, ch, :]) for ch in range(CH)],
                       sem_name=smm)
            p.wait("vector", smm, tmm)
            trow = p.op("vector", "tensor_copy", rQ[0:1, tb * TS:(tb + 1) * TS],
                        prow[0:1, 0:TS], sem_name=sdve)
            p.set_guard(pfx + "prow", sdve, trow)
            p.wait("sync", sdve, trow)
            tq = p.dma("sync", qr[:, :, t0 + tb * TS:t0 + (tb + 1) * TS],
                       qst[:, :, :], pfx + "qst")
            p.set_guard(pfx + "qst_dma", pfx + "qst", tq)
            if tb == 3:
                p.set_guard(pfx + "esq", pfx + "qst", tq)
            for dc in range(CH):
                bank = pj[dc % 2]
                p.wait_buf("tensor", pfx + f"pjB{dc % 2}")
                tmm = p.mm(bank[:, 0:TS],
                           [(wt_ap(W["k"], ch, dc), yt[:, ch, :])
                            for ch in range(CH)], sem_name=smm)
                p.wait("vector", smm, tmm)
                td = p.op("vector", "tensor_scalar",
                          KT[:, dc, tb * TS:(tb + 1) * TS],
                          bank[:, 0:TS], bk[:, dc:dc + 1], None, ALU.add,
                          sem_name=sdve)
                p.set_guard(pfx + f"pjB{dc % 2}", sdve, td)
                p.wait("scalar", sdve, td)
                ta = p.op("scalar", "activation", sqs[:, dc, :],
                          KT[:, dc, tb * TS:(tb + 1) * TS],
                          AF.Square, sem_name=sact)
            p.wait("tensor", sact, p.val(sact))
            p.wait_buf("tensor", pfx + "prow")
            tmm = p.mm(prow[0:1, 0:TS],
                       [(p.onescb[:, :], sqs[:, ch, :]) for ch in range(CH)],
                       sem_name=smm)
            p.wait("vector", smm, tmm)
            trow = p.op("vector", "tensor_copy", rK[0:1, tb * TS:(tb + 1) * TS],
                        prow[0:1, 0:TS], sem_name=sdve)
            p.set_guard(pfx + "prow", sdve, trow)
            for tc in range(4):
                p.wait_buf("tensor", pfx + "pcol")
                tmm = p.mm(pcol[:, 0:1],
                           [(sqs[:, ch, tc * 128:(tc + 1) * 128], p.onescb[:, :])
                            for ch in range(CH)], sem_name=smm)
                p.wait("vector", smm, tmm)
                tkc = p.op("vector", "tensor_copy",
                           knc[:, tb * 4 + tc:tb * 4 + tc + 1],
                           pcol[:, 0:1], sem_name=sdve)
                p.set_guard(pfx + "pcol", sdve, tkc)
            tsq_last = p.val(smm)
            p.set_guard(pfx + "at_sq", smm, tsq_last)
            for tc in range(4):
                for dh in range(2):
                    bank = pj[dh]
                    p.wait_buf("tensor", pfx + f"pjB{dh}")
                    pairs = [(yt[:, ch, tc * 128:(tc + 1) * 128],
                              W["v"][:, ch, dh * TS:(dh + 1) * TS])
                             for ch in range(CH)]
                    pairs.append((p.ones1b[:, :], bvr[0:1, dh * TS:(dh + 1) * TS]))
                    tmm = p.mm(bank[:, 0:TS], pairs, sem_name=smm)
                    p.wait("vector", smm, tmm)
                    td = p.op("vector", "tensor_copy",
                              Vs[:, tb * 4 + tc, dh * TS:(dh + 1) * TS],
                              bank[:, 0:TS], sem_name=sdve)
                    p.set_guard(pfx + f"pjB{dh}", sdve, td)
        if dbg == "y":
            continue
        if dbg == "k":
            p.wait("gpsimd", sdve, p.val(sdve))
            p.dma("gpsimd",
                  qt_dram.rearrange("(p a) t -> p a t", p=128)[:, 5, 0:1024],
                  KT[:, :, 1920:2048], pfx + "edmp")
            p.wait("gpsimd", pfx + "edmp", p.val(pfx + "edmp"))
            continue
        p.set_guard(pfx + "yt", smm, p.val(smm))
        if upto < 2:
            continue
        p.wait_buf("gpsimd", pfx + "Wq_inuse")
        p.wait_buf("gpsimd", pfx + "Wq")
        p.dma("gpsimd", W["q"][:, :, :],
              w_in["a0o"].rearrange("(ch p) d -> p ch d", p=128), pfx + "wlg")
        two = p.val(pfx + "wlg")
        p.wait("scalar", sdve, p.val(sdve))
        p.op("scalar", "activation", rQ, rQ, AF.Sqrt)
        ta = p.op("scalar", "activation", rK, rK, AF.Sqrt, sem_name=sact)
        p.wait("vector", sact, ta)
        p.op("vector", "reduce_max", mx[0:1, 0:1], rQ, mybir.AxisListType.X)
        tv = p.op("vector", "reduce_max", mx[0:1, 1:2], rK, mybir.AxisListType.X,
                  sem_name=sdve)
        p.wait("scalar", sdve, tv)
        p.op("scalar", "activation", rQ, rQ, AF.Ln, bias=p.const_p(1e-8, 1))
        ta = p.op("scalar", "activation", rQ, rQ, AF.Exp, scale=-1.0,
                  sem_name=sact)
        p.wait("vector", sact, ta)
        tv = p.op("vector", "scalar_tensor_tensor", mx[0:1, 2:3], mx[0:1, 0:1],
                  -1.0 / 32.0, mx[0:1, 1:2], ALU.mult, ALU.mult, sem_name=sdve)
        p.wait("tensor", sdve, tv)
        p.wait_buf("tensor", pfx + "pcol")
        tmm = p.mm(pcol[:, 0:1], [(p.ones1f[:, :], mx[0:1, 2:3])], sem_name=smm)
        p.wait("vector", smm, tmm)
        tnm = p.op("vector", "tensor_copy", nmc[:, :], pcol[:, 0:1],
                   sem_name=sdve)
        p.set_guard(pfx + "pcol", sdve, tnm)
        p.wait("scalar", sdve, p.val(sdve))
        p.op("scalar", "activation", knc[:, :], knc[:, :], AF.Sqrt)
        p.op("scalar", "activation", knc[:, :], knc[:, :], AF.Ln,
             bias=p.const(1e-8))
        tk = p.op("scalar", "activation", knc[:, :], knc[:, :], AF.Exp,
                  scale=-1.0, sem_name=sact)
        if upto < 3:
            continue
        if dbg == "k2":
            p.wait("gpsimd", sact, p.val(sact))
            p.wait("gpsimd", sdve, p.val(sdve))
            p.dma("gpsimd",
                  qt_dram.rearrange("(p a) t -> p a t", p=128)[:, 5, 0:1024],
                  KT[:, :, 1920:2048], pfx + "edmp")
            p.wait("gpsimd", pfx + "edmp", p.val(pfx + "edmp"))
            continue
        p.wait("tensor", pfx + "wlg", two)
        for qb in (range(1) if (dbg and dbg not in ("y", "k", "k2")) else range(4)):
            p.wait_buf("sync", pfx + "yt")
            p.wait_buf("sync", pfx + "Qq")
            qtg = p.dma("sync", Qq[:, :, :],
                        qr[:, :, t0 + qb * TS:t0 + (qb + 1) * TS], pfx + "qld")
            p.wait("tensor", sact, p.val(sact))
            p.wait("tensor", sdve, p.val(sdve))
            p.wait_buf("tensor", pfx + "pcol")
            tmm = p.mm(pcol[:, 0:TS],
                       [(p.ones1f[:, :], rQ[0:1, qb * TS:(qb + 1) * TS])],
                       sem_name=smm)
            p.wait("vector", smm, tmm)
            p.wait_buf("vector", pfx + "rqb")
            trq = p.op("vector", "tensor_copy", rqb[:, :], pcol[:, 0:TS],
                       sem_name=sdve)
            p.set_guard(pfx + "pcol", sdve, trq)
            if dbg == "e":
                p.wait("gpsimd", sdve, trq)
                p.dma("gpsimd",
                      qt_dram.rearrange("(p a) t -> p a t", p=128)[:, 2, 0:TS],
                      rqb[:, :], pfx + "edmp")
                p.wait("gpsimd", pfx + "act", p.val(pfx + "act"))
                p.dma("gpsimd",
                      qt_dram.rearrange("(p a) t -> p a t", p=128)[:, 3, 0:16],
                      knc[:, :], pfx + "edmp")
            p.wait("tensor", pfx + "qld", qtg)
            if qb == 0:
                p.wait_buf("scalar", pfx + "esq")
            for kt in range(16):
                bank = pj[kt % 2]
                p.wait_buf("tensor", pfx + f"pjB{kt % 2}")
                tmm = p.mm(bank[:, 0:TS],
                           [(KT[:, ch, kt * 128:(kt + 1) * 128], Qq[:, ch, :])
                            for ch in range(CH)], sem_name=smm)
                if kt == 15:
                    p.set_guard(pfx + "Qq", smm, tmm)
                p.wait("vector", smm, tmm)
                p.wait_buf("vector", pfx + "scP")
                p.op("vector", "tensor_scalar", sc1[:, :], rqb[:, :],
                     knc[:, kt:kt + 1], -0.5, ALU.mult, ALU.mult)
                p.op("vector", "tensor_tensor", sc1[:, :], bank[:, 0:TS],
                     sc1[:, :], ALU.mult)
                p.op("vector", "tensor_scalar", sc2[:, :], sc1[:, :], 0.25,
                     None, ALU.add)
                p.op("vector", "tensor_scalar", sc1[:, :], sc2[:, :], MAGIC,
                     MAGIC, ALU.add, ALU.subtract)
                td = p.op("vector", "tensor_tensor", sc2[:, :], sc2[:, :],
                          sc1[:, :], ALU.subtract, sem_name=sdve)
                p.wait("scalar", sdve, td)
                ta = p.op("scalar", "activation", sc1[:, :], sc2[:, :], AF.Sin,
                          scale=float(2 * math.pi), sem_name=sact)
                p.wait("vector", sact, ta)
                td = p.op("vector", "scalar_tensor_tensor", sc2[:, :],
                          bank[:, 0:TS], 1.0 / 32.0, sc1[:, :],
                          ALU.mult, ALU.mult, sem_name=sdve)
                p.set_guard(pfx + f"pjB{kt % 2}", sdve, td)
                if dbg in ("e", "e2") and kt == 15 and b == 0 and qb == 0:
                    p.wait("gpsimd", sdve, td)
                    p.dma("gpsimd",
                          qt_dram.rearrange("(p a) t -> p a t", p=128)[:, 1, 0:TS],
                          sc2[:, :], pfx + "edmp")
                    p.wait("vector", pfx + "edmp", p.val(pfx + "edmp"))
                p.wait("scalar", sdve, td)
                ta = p.op("scalar", "activation", es[:, kt, :], sc2[:, :],
                          AF.Exp, bias=nmc[:, 0:1], sem_name=sact)
                p.set_guard(pfx + "scP", sact, ta)
            if dbg in ("k3", "k4", "k5", "k6", "k6a", "k6b") and qb == 0:
                p.wait("gpsimd", smm, p.val(smm))
                p.wait("gpsimd", sdve, p.val(sdve))
                p.wait("gpsimd", sact, p.val(sact))
                p.dma("gpsimd",
                      qt_dram.rearrange("(p a) (c k) -> p a c k", p=128, c=4)
                      [:, 4, :, :],
                      KT[:, 0:4, :], pfx + "edmp")
                p.dma("gpsimd",
                      qt_dram.rearrange("(p a) (c k) -> p a c k", p=128, c=4)
                      [:, 5, :, :],
                      KT[:, 4:8, :], pfx + "edmp")
                p.dma("gpsimd",
                      qt_dram.rearrange("(p a) (c k) -> p a c k", p=128, c=8)
                      [:, 6, :, :],
                      Vs[:, 0:8, :], pfx + "edmp")
                p.dma("gpsimd",
                      qt_dram.rearrange("(p a) (c k) -> p a c k", p=128, c=8)
                      [:, 7, :, :],
                      Vs[:, 8:16, :], pfx + "edmp")
                p.wait("gpsimd", pfx + "edmp", p.val(pfx + "edmp"))
            if dbg in ("e", "e2") and b == 0 and qb == 0:
                p.wait("sync", sact, p.val(sact))
                p.dma("sync", qt_dram.rearrange("(p a) t -> p a t", p=128)[:, 0, 0:16 * TS],
                      es[:, :, :].rearrange("p k t -> p (k t)"), pfx + "edmp")
                p.dma("sync", qt_dram.rearrange("(p a) t -> p a t", p=128)[:, 2, 0:16 * TS],
                      es[:, :, :].rearrange("p k t -> p (k t)"), pfx + "edmp")
                p.wait("sync", pfx + "edmp", p.val(pfx + "edmp"))
            if upto < 4:
                continue
            p.wait("tensor", sact, p.val(sact))
            p.wait_buf("tensor", pfx + "prow")
            tmm = p.mm(prow[0:1, 0:TS],
                       [(p.onescb[:, :], es[:, kt, :]) for kt in range(16)],
                       sem_name=smm)
            p.wait("scalar", smm, tmm)
            p.op("scalar", "activation", rr_, prow[0:1, 0:TS], AF.Ln)
            ta = p.op("scalar", "activation", rr_, rr_, AF.Exp, scale=-1.0,
                      sem_name=sact)
            p.set_guard(pfx + "prow", sact, ta)
            p.wait("tensor", sact, ta)
            p.wait_buf("tensor", pfx + "pcol")
            tmm = p.mm(pcol[:, 0:TS], [(p.ones1f[:, :], rr_)], sem_name=smm)
            p.wait("vector", smm, tmm)
            p.wait_buf("vector", pfx + "rsb")
            trs = p.op("vector", "tensor_copy", rsb[:, :], pcol[:, 0:TS],
                       sem_name=sdve)
            p.set_guard(pfx + "pcol", sdve, trs)
            if upto < 5:
                continue
            p.wait_buf("vector", pfx + "at_dma")
            for dc in range(CH):
                bank = pj[dc % 2]
                p.wait_buf("tensor", pfx + f"pjB{dc % 2}")
                tmm = p.mm(bank[:, 0:TS],
                           [(Vs[:, kt, dc * 128:(dc + 1) * 128], es[:, kt, :])
                            for kt in range(16)], sem_name=smm)
                p.wait("vector", smm, tmm)
                td = p.op("vector", "tensor_tensor", at[:, dc, :], bank[:, 0:TS],
                          rsb[:, :], ALU.mult, sem_name=sdve)
                p.set_guard(pfx + f"pjB{dc % 2}", sdve, td)
            p.wait("sync", sdve, td)
            tst = p.dma("sync", ar[:, :, t0 + qb * TS:t0 + (qb + 1) * TS],
                        at[:, :, :], pfx + "ast")
            p.set_guard(pfx + "at_dma", pfx + "ast", tst)
        if upto < 6:
            continue
        for tb in range(4):
            p.wait_buf("sync", pfx + "xt")
            tgt = p.dma("sync", xt[:, :, :],
                        src_r[:, :, t0 + tb * TS:t0 + (tb + 1) * TS], pfx + "xin")
            p.wait_buf("gpsimd", pfx + "at_ld")
            p.wait_buf("gpsimd", pfx + "at_dma")
            tat = p.dma("gpsimd", at[:, :, :],
                        ar[:, :, t0 + tb * TS:t0 + (tb + 1) * TS], pfx + "atl")
            p.wait("tensor", pfx + "atl", tat)
            p.wait("vector", pfx + "xin", tgt)
            p.wait_buf("vector", "ln_tscr")
            for ec in range(CH):
                bank = pj[ec % 2]
                p.wait_buf("tensor", pfx + f"pjB{ec % 2}")
                tmm = p.mm(bank[:, 0:TS],
                           [(wt_ap(W["q"], ch, ec), at[:, ch, :])
                            for ch in range(CH)], sem_name=smm)
                p.wait("vector", smm, tmm)
                p.op("vector", "tensor_scalar", ot[:, ec, :], bank[:, 0:TS],
                     bo[:, ec:ec + 1], None, ALU.add)
                td = p.op("vector", "tensor_tensor", ot[:, ec, :], ot[:, ec, :],
                          xt[:, ec, :], ALU.add, sem_name=sdve)
                p.set_guard(pfx + f"pjB{ec % 2}", sdve, td)
            p.set_guard(pfx + "xt", sdve, td)
            p.set_guard(pfx + "at_ld", smm, p.val(smm))
            p.wait("sync", sdve, td)
            tso = p.dma("sync", dst_r[:, :, t0 + tb * TS:t0 + (tb + 1) * TS],
                        ot[:, :, :], pfx + "xo")
            p.set_guard("ln_tscr", pfx + "xo", tso)
            p.wait_buf("vector", "ln_tscr_next")
        p.set_guard(pfx + "Wq", smm, p.val(smm))
        p.set_guard(pfx + "Wk", smm, p.val(smm))
        p.set_guard(pfx + "Wv", smm, p.val(smm))
    p.wait("sync", pfx + "xo", p.val(pfx + "xo"))
    for n in ["pcol", "prow", "pj1", "pj0"]:
        p.sfree(n if n.startswith(pfx) else pfx + n)
    for n in ["at", "mrow", "sc2", "sc1", "es", "rqb", "nmc", "knc",
              "rows", "yt"]:
        p.sfree(pfx + n)
    for n in ["lnrows2", "lnrows1", "lntscr", "lnxb"]:
        p.sfree(pfx + n)
    for n in ["psBB", "psRB", "psQ", "psS"]:
        p.sfree(pfx + n)
    for n in ["xt", "Vs", "KT", "bvr", "bo", "bk", "bq", "Wv", "Wk", "Wq"]:
        p.sfree(pfx + n)



def build_l0():
    p = Prog("pl0")
    nc = p.nc
    x_in = nc.dram_tensor("x", [C, TOK], F32, kind="ExternalInput").ap()
    x_out = nc.dram_tensor("xo", [C, TOK], F32, kind="ExternalOutput").ap()
    names = (["f0r", "f0k", "f0v", "a0q", "a0k", "a0v", "a0o"])
    w_in = {n: nc.dram_tensor(n, [C, C], F32, kind="ExternalInput").ap()
            for n in names}
    for n in ["a0bq", "a0bk", "a0bo"]:
        w_in[n] = nc.dram_tensor(n, [128, CH], F32, kind="ExternalInput").ap()
    w_in["a0bv"] = nc.dram_tensor("a0bv", [1, C], F32, kind="ExternalInput").ap()
    for n in ["mur", "muk"]:
        w_in[n] = nc.dram_tensor(n, [128, CH], F32, kind="ExternalInput").ap()
    xb1 = nc.dram_tensor("xb1", [C, TOK], F32, kind="Internal").ap()
    xb2 = nc.dram_tensor("xb2", [C, TOK], F32, kind="Internal").ap()
    qt_dram = nc.dram_tensor("qtd", [C, TOK], BF16, kind="Internal").ap()
    at_dram = nc.dram_tensor("atd", [C, TOK], BF16, kind="Internal").ap()

    xr_in = x_in.rearrange("(ch p) t -> p ch t", p=128)
    r1 = xb1.rearrange("(ch p) t -> p ch t", p=128)
    r2 = xb2.rearrange("(ch p) t -> p ch t", p=128)
    r_out = x_out.rearrange("(ch p) t -> p ch t", p=128)

    mur = p.sbuf("s_mur", [128, CH], F32)
    omr = p.sbuf("s_omr", [128, CH], F32)
    muk = p.sbuf("s_muk", [128, CH], F32)
    omk = p.sbuf("s_omk", [128, CH], F32)

    p.init_stage()
    p.dma("sync", mur[:, :], w_in["mur"][:, :], "mul")
    p.dma("sync", muk[:, :], w_in["muk"][:, :], "mul")
    p.wait("vector", "mul", p.val("mul"))
    p.op("vector", "tensor_scalar", omr[:, :], mur[:, :], -1.0, 1.0,
         ALU.mult, ALU.add)
    p.op("vector", "tensor_scalar", omk[:, :], muk[:, :], -1.0, 1.0,
         ALU.mult, ALU.add)

    p.stage("f0a")
    emit_f0(p, "fa", xr_in, r1, w_in, mur, omr, muk, omk)
    p.stage("a0")
    emit_a0(p, r1, r2, w_in, qt_dram, at_dram)
    p.stage("f0b")
    emit_f0(p, "fb", r2, r_out, w_in, mur, omr, muk, omk)
    return p.finalize()





FF = 4096
NQ = 8
C2 = float(0.5 / math.sqrt(128.0))


def emit_a1(p, src_r, dst_r, w_in, dram):
    TS = 512
    pfx = "m"
    smm, sdve, sact = pfx + "mm", pfx + "dve", pfx + "act"
    qt_d, kt_d, v_d, at_d, x1f_d, x1b_d, y_d = (
        dram["qt"], dram["kt"], dram["v"], dram["at"], dram["x1f"],
        dram["x1b"], dram["y"])
    yr_d = y_d.rearrange("(ch p) t -> p ch t", p=128)
    ar = at_d.rearrange("(ch p) t -> p ch t", p=128)
    x1fr = x1f_d.rearrange("(ch p) t -> p ch t", p=128)
    x1br = x1b_d.rearrange("(ch p) t -> p ch t", p=128)

    bq = p.sbuf(pfx + "bq", [128, CH], F32)
    bk = p.sbuf(pfx + "bk", [128, CH], F32)
    bk2 = p.sbuf(pfx + "bk2", [128, CH], F32)
    bo = p.sbuf(pfx + "bo", [128, CH], F32)
    g1 = p.sbuf(pfx + "g1", [128, CH], F32)
    be1 = p.sbuf(pfx + "be1", [128, CH], F32)
    g2 = p.sbuf(pfx + "g2", [128, CH], F32)
    be2 = p.sbuf(pfx + "be2", [128, CH], F32)
    b1c = p.sbuf(pfx + "b1c", [128, 32], F32)
    bqoc = p.sbuf(pfx + "bqoc", [128, 32], F32)
    b2c = p.sbuf(pfx + "b2c", [128, CH], F32)
    bc2 = p.sbuf(pfx + "bc2", [8, 8], F32)
    bmc = p.sbuf(pfx + "bmc", [8, 1], F32)
    bvr = p.sbuf(pfx + "bvr", [1, C], BF16)
    wc = p.sbuf(pfx + "wc", [128, CH, NQ], BF16)
    wm = p.sbuf(pfx + "wm", [128, 32, NQ], BF16)
    for n, t in [("a1bq", bq), ("a1bk", bk), ("a1bo", bo), ("a1g1", g1),
                 ("a1be1", be1), ("a1g2", g2), ("a1be2", be2), ("a1b1", b1c),
                 ("a1bqo", bqoc), ("a1b2", b2c), ("a1bc2", bc2), ("a1bm", bmc)]:
        p.dma("sync", t[:, :], w_in[n][:, :], pfx + "wl")
    p.dma("gpsimd", bvr[:, :], w_in["a1bv"][:, :], pfx + "wlg")
    p.dma("gpsimd", wc[:, :, :],
          w_in["a1wc"].rearrange("(h p) n -> p h n", p=128), pfx + "wlg")
    p.dma("gpsimd", wm[:, :, :],
          w_in["a1wm"].rearrange("(fc p) n -> p fc n", p=128), pfx + "wlg")
    p.wait("vector", pfx + "wl", p.val(pfx + "wl"))
    p.op("vector", "tensor_scalar", bk2[:, :], bk[:, :], C2, None, ALU.mult)

    W = {n: p.sbuf(pfx + "W" + n, [128, CH, C], BF16) for n in ["q", "k", "v"]}
    xt = p.sbuf(pfx + "xt", [128, CH, TS], F32)
    ps = make_ln_scratch_sb(p, TS, pfx)
    yt = p.sbuf(pfx + "yt", [128, CH, TS], BF16)
    qst = p.sbuf(pfx + "qst", [128, CH, TS], BF16)
    vst = p.sbuf(pfx + "vst", [128, C], BF16)
    QTh = p.sbuf(pfx + "QTh", [128, SL], BF16)
    KTh = p.sbuf(pfx + "KTh", [128, SL], BF16)
    Vh = p.sbuf(pfx + "Vh", [128, 16, 128], BF16)
    qq = p.sbuf(pfx + "qq", [8, SL], F32)
    kq = p.sbuf(pfx + "kq", [8, SL], F32)
    qqn = p.sbuf(pfx + "qqn", [8, SL], BF16)
    qqh = p.sbuf(pfx + "qqh", [8, SL], BF16)
    kqn = p.sbuf(pfx + "kqn", [8, SL], BF16)
    sq8 = p.sbuf(pfx + "sq8", [8, TS], F32)
    sqb = p.sbuf(pfx + "sqb", [128, TS], BF16)
    rows = p.sbuf(pfx + "rows", [128, SL], F32)
    mrow = p.sbuf(pfx + "mrow", [1, 8], F32)
    nmc = p.sbuf(pfx + "nmc", [128, 1], F32)
    rsb = p.sbuf(pfx + "rsb", [128, TS], F32)
    es = p.sbuf(pfx + "es", [128, 16, TS], BF16)
    scs = p.sbuf(pfx + "scs", [128, TS], F32)
    sct = p.sbuf(pfx + "sct", [128, TS], F32)
    x1f = p.sbuf(pfx + "x1f", [128, CH, TS], F32)
    pA = p.psum(pfx + "pA")
    pB = p.psum(pfx + "pB")
    pC = p.psum(pfx + "pC")
    pD = p.psum(pfx + "pD")
    pE = p.psum(pfx + "pE")
    pF = p.psum(pfx + "pF")
    ps["psS"], ps["psQ"], ps["psRB"], ps["psBB"] = pA, pB, pC, pD
    ps["gk"] = {"S": "gA", "Q": "gB", "RB": "gC", "BB": "gD",
                "xb": pfx + "lnxb_g", "tscr": pfx + "lntscr_g"}
    rQ = rows[0:1, 0:SL]
    rK = rows[0:1, 0:SL]
    rX = rows[32:33, 0:SL]
    x1p = ps["tscr"]
    pj = [pE, pF]
    pjk = ["gE", "gF"]
    pab = [pC, pD]
    pabk = ["gC", "gD"]

    def mmg(i, cols, pairs, banks=pj, keys=pjk, rowlim=None):
        k = keys[i % 2]
        p.wait_buf("tensor", k)
        bank = banks[i % 2]
        ap = bank[0:rowlim, cols] if rowlim else bank[:, cols]
        return p.mm(ap, pairs, sem_name=smm), bank, k

    for b in range(4):
        t0 = b * SL
        for n, key in [("q", "a1wq"), ("k", "a1wk"), ("v", "a1wv")]:
            p.wait_buf("gpsimd", pfx + "W" + n)
            p.dma("gpsimd", W[n][:, :, :],
                  w_in[key].rearrange("(ch p) d -> p ch d", p=128), pfx + "wlg")
        p.wait("tensor", pfx + "wlg", p.val(pfx + "wlg"))
        for tb in range(4):
            p.wait_buf("sync", pfx + "xt")
            tgt = p.dma("sync", xt[:, :, :],
                        src_r[:, :, t0 + tb * TS:t0 + (tb + 1) * TS], pfx + "xin")
            p.wait("vector", pfx + "xin", tgt)
            p.wait_buf("vector", pfx + "yf")
            ytail = emit_ln_tile(p, ps, xt[:, :, :], x1f[:, :, :], TS)
            p.set_guard(pfx + "xt", "lndve", ytail)
            p.wait_buf("vector", pfx + "yt")
            tyc = p.op("vector", "tensor_copy", yt[:, :, :], x1f[:, :, :],
                       sem_name=sdve)
            p.wait("sync", sdve, tyc)
            tyd = p.dma("sync", yr_d[:, :, t0 + tb * TS:t0 + (tb + 1) * TS],
                        x1f[:, :, :], pfx + "qsd")
            p.set_guard(pfx + "yf", pfx + "qsd", tyd)
            p.wait("tensor", sdve, tyc)
            p.wait_buf("tensor", pfx + "qst_dma")
            last = None
            for dc in range(CH):
                tmm, bank, bk_ = mmg(dc, slice(0, TS),
                                     [(wt_ap(W["q"], ch, dc), yt[:, ch, :])
                                      for ch in range(CH)])
                if dc == CH - 1:
                    p.set_guard(pfx + "Wq_inuse", smm, tmm)
                p.wait("vector", smm, tmm)
                last = p.op("vector", "tensor_scalar", qst[:, dc, :],
                            bank[:, 0:TS], bq[:, dc:dc + 1], None, ALU.add,
                            sem_name=sdve)
                p.set_guard(bk_, sdve, last)
            p.wait("sync", sdve, last)
            tq = p.dma("sync", qt_d[:, tb * TS:(tb + 1) * TS]
                       .rearrange("(ch p) t -> p ch t", p=128),
                       qst[:, :, :], pfx + "qsd")
            p.set_guard(pfx + "qst_dma", pfx + "qsd", tq)
            p.wait_buf("tensor", pfx + "qst_dma")
            for dc in range(CH):
                tmm, bank, bk_ = mmg(dc, slice(0, TS),
                                     [(wt_ap(W["k"], ch, dc), yt[:, ch, :])
                                      for ch in range(CH)])
                p.wait("vector", smm, tmm)
                last = p.op("vector", "tensor_scalar", qst[:, dc, :],
                            bank[:, 0:TS], C2, bk2[:, dc:dc + 1],
                            ALU.mult, ALU.add, sem_name=sdve)
                p.set_guard(bk_, sdve, last)
            p.wait("sync", sdve, last)
            tq = p.dma("sync", kt_d[:, tb * TS:(tb + 1) * TS]
                       .rearrange("(ch p) t -> p ch t", p=128),
                       qst[:, :, :], pfx + "qsd")
            p.set_guard(pfx + "qst_dma", pfx + "qsd", tq)
            for tc in range(4):
                p.wait_buf("sync", pfx + "vst_dma")
                for dh in range(2):
                    pairs = [(yt[:, ch, tc * 128:(tc + 1) * 128],
                              W["v"][:, ch, dh * TS:(dh + 1) * TS])
                             for ch in range(CH)]
                    pairs.append((p.ones1b[:, :],
                                  bvr[0:1, dh * TS:(dh + 1) * TS]))
                    tmm, bank, bk_ = mmg(dh, slice(0, TS), pairs)
                    p.wait("vector", smm, tmm)
                    last = p.op("vector", "tensor_copy",
                                vst[:, dh * TS:(dh + 1) * TS], bank[:, 0:TS],
                                sem_name=sdve)
                    p.set_guard(bk_, sdve, last)
                p.wait("sync", sdve, last)
                tv = p.dma("sync",
                           v_d[tb * TS + tc * 128:tb * TS + tc * 128 + 128, :],
                           vst[:, :], pfx + "qsd")
                p.set_guard(pfx + "vst_dma", pfx + "qsd", tv)
            p.set_guard(pfx + "yt", smm, p.val(smm))
        p.wait_buf("gpsimd", pfx + "Wq_inuse")
        p.wait_buf("gpsimd", pfx + "Wq")
        p.dma("gpsimd", W["q"][:, :, :],
              w_in["a1wo"].rearrange("(ch p) d -> p ch d", p=128), pfx + "wlg")
        two = p.val(pfx + "wlg")
        for h in range(8):
            p.wait_buf("sync", pfx + "QTh")
            tqh = p.dma("sync", QTh[:, :], qt_d[h * 128:(h + 1) * 128, :],
                        pfx + "hld")
            p.wait_buf("sync", pfx + "KTh")
            tkh = p.dma("sync", KTh[:, :], kt_d[h * 128:(h + 1) * 128, :],
                        pfx + "hld")
            p.wait_buf("gpsimd", pfx + "Vh")
            if h == 0:
                p.wait("gpsimd", pfx + "qsd", p.val(pfx + "qsd"))
            tvh = p.dma("gpsimd", Vh[:, :, :],
                        v_d.rearrange("(kt p) d -> p kt d", p=128)
                        [:, :, h * 128:(h + 1) * 128], pfx + "vld")
            p.wait("tensor", pfx + "hld", tqh)
            p.wait_buf("vector", pfx + "qq")
            for qb in range(4):
                tmm, bank, bk_ = mmg(qb, slice(0, TS),
                                     [(wc[:, h, :], QTh[:, qb * TS:(qb + 1) * TS])],
                                     banks=pab, keys=pabk, rowlim=8)
                p.wait("vector", smm, tmm)
                td = p.op("vector", "tensor_scalar",
                          qq[:, qb * TS:(qb + 1) * TS], bank[0:8, 0:TS],
                          bc2[:, h:h + 1], None, ALU.add, sem_name=sdve)
                p.set_guard(bk_, sdve, td)
            p.wait("tensor", pfx + "hld", tkh)
            p.wait_buf("vector", pfx + "kq")
            for qb in range(4):
                tmm, bank, bk_ = mmg(qb, slice(0, TS),
                                     [(wc[:, h, :], KTh[:, qb * TS:(qb + 1) * TS])],
                                     banks=pab, keys=pabk, rowlim=8)
                p.wait("vector", smm, tmm)
                td = p.op("vector", "tensor_scalar",
                          kq[:, qb * TS:(qb + 1) * TS], bank[0:8, 0:TS],
                          bc2[:, h:h + 1], None, ALU.add, sem_name=sdve)
                p.set_guard(bk_, sdve, td)
            for src_t, dsts in [(qq, [(qqn, None), (qqh, 0.5)]),
                                (kq, [(kqn, None)])]:
                p.wait_buf("vector", pfx + "rrow")
                for qb in range(4):
                    p.wait("scalar", sdve, p.val(sdve))
                    p.wait_buf("scalar", pfx + "sq8")
                    ta = p.op("scalar", "activation", sq8[:, :],
                              src_t[:, qb * TS:(qb + 1) * TS], AF.Square,
                              sem_name=sact)
                    p.wait("tensor", sact, ta)
                    p.wait_buf("tensor", "gA")
                    tmm = p.mm(pA[0:1, 0:TS],
                               [(p.onescf[0:8, :], sq8[:, :])], sem_name=smm)
                    p.set_guard(pfx + "sq8", smm, tmm)
                    p.wait("vector", smm, tmm)
                    td = p.op("vector", "tensor_copy",
                              rQ[0:1, qb * TS:(qb + 1) * TS], pA[0:1, 0:TS],
                              sem_name=sdve)
                    p.set_guard("gA", sdve, td)
                p.wait("scalar", sdve, p.val(sdve))
                p.op("scalar", "activation", rQ, rQ, AF.Ln)
                trk = p.op("scalar", "activation", rQ, rQ, AF.Exp, scale=-0.5,
                           sem_name=sact)
                p.wait("tensor", sact, trk)
                for qb in range(4):
                    sl = slice(qb * TS, (qb + 1) * TS)
                    p.wait_buf("tensor", "gB")
                    tmm = p.mm(pB[0:8, 0:TS],
                               [(p.ones1f[0:1, 0:8], rQ[0:1, sl])],
                               sem_name=smm)
                    p.set_guard(pfx + "rrow", smm, tmm)
                    p.wait("vector", smm, tmm)
                    td = None
                    for dst_t, scale in dsts:
                        if scale is None:
                            td = p.op("vector", "tensor_tensor", dst_t[:, sl],
                                      src_t[:, sl], pB[0:8, 0:TS], ALU.mult,
                                      sem_name=sdve)
                        else:
                            td = p.op("vector", "scalar_tensor_tensor",
                                      dst_t[:, sl], src_t[:, sl], scale,
                                      pB[0:8, 0:TS], ALU.mult, ALU.mult,
                                      sem_name=sdve)
                    p.set_guard("gB", sdve, td)
            for i, srcT in enumerate([QTh, KTh]):
                for qb in range(4):
                    sl = slice(qb * TS, (qb + 1) * TS)
                    p.wait_buf("scalar", pfx + "sqb")
                    ta = p.op("scalar", "activation", sqb[:, :], srcT[:, sl],
                              AF.Square, sem_name=sact)
                    p.wait("tensor", sact, ta)
                    p.wait_buf("tensor", "gA")
                    tmm = p.mm(pA[0:1, 0:TS],
                               [(p.onescb[:, :], sqb[:, :])], sem_name=smm)
                    p.set_guard(pfx + "sqb", smm, tmm)
                    p.wait("vector", smm, tmm)
                    td = p.op("vector", "tensor_copy", rX[0:1, sl],
                              pA[0:1, 0:TS], sem_name=sdve)
                    p.set_guard("gA", sdve, td)
                tv2 = p.op("vector", "reduce_max", mrow[0:1, i:i + 1], rX,
                           mybir.AxisListType.X, sem_name=sdve)
            p.wait("scalar", sdve, tv2)
            ta = p.op("scalar", "activation", mrow[0:1, 0:2], mrow[0:1, 0:2],
                      AF.Sqrt, sem_name=sact)
            p.wait("vector", sact, ta)
            p.op("vector", "scalar_tensor_tensor", mrow[0:1, 2:3],
                 mrow[0:1, 0:1], -1.0, mrow[0:1, 1:2], ALU.mult, ALU.mult)
            tv2 = p.op("vector", "tensor_scalar", mrow[0:1, 2:3],
                       mrow[0:1, 2:3], -0.55, None, ALU.add, sem_name=sdve)
            p.wait("tensor", sdve, tv2)
            p.wait_buf("tensor", "gB")
            tmm = p.mm(pB[:, 0:1], [(p.ones1f[:, :], mrow[0:1, 2:3])],
                       sem_name=smm)
            p.wait("vector", smm, tmm)
            p.wait_buf("vector", pfx + "nmc")
            tnm = p.op("vector", "tensor_copy", nmc[:, :], pB[:, 0:1],
                       sem_name=sdve)
            p.set_guard("gB", sdve, tnm)
            for qb in range(4):
                sl = slice(qb * TS, (qb + 1) * TS)
                for kt in range(16):
                    ksl = slice(kt * 128, (kt + 1) * 128)
                    tA, bankA, kA = mmg(kt, slice(0, TS),
                                        [(kqn[:, ksl], qqn[:, sl])],
                                        banks=pab, keys=pabk, rowlim=128)
                    tB, bankB, kB = mmg(kt, slice(0, TS),
                                        [(KTh[:, ksl], QTh[:, sl]),
                                         (kqn[:, ksl], qqh[:, sl])])
                    if kt >= 14 and qb == 3:
                        p.set_guard(pfx + ("KTh" if kt == 15 else "Vh_pre"),
                                    smm, tB)
                    p.wait("scalar", smm, tA)
                    p.wait_buf("scalar", pfx + "scs")
                    ta = p.op("scalar", "activation", scs[:, :],
                              bankA[:, 0:TS], AF.Sin, scale=float(math.pi),
                              sem_name=sact)
                    p.wait("vector", sact, ta)
                    p.wait("vector", smm, tB)
                    p.wait_buf("vector", pfx + "sct")
                    p.op("vector", "scalar_tensor_tensor", sct[:, :],
                         scs[:, :], 0.05, bankA[:, 0:TS], ALU.mult, ALU.mult)
                    td = p.op("vector", "tensor_tensor", sct[:, :], sct[:, :],
                              bankB[:, 0:TS], ALU.add, sem_name=sdve)
                    p.set_guard(kA, sdve, td)
                    p.set_guard(kB, sdve, td)
                    p.set_guard(pfx + "scs", sdve, td)
                    p.wait("scalar", sdve, td)
                    ta = p.op("scalar", "activation", es[:, kt, :], sct[:, :],
                              AF.Exp, bias=nmc[:, 0:1], sem_name=sact)
                    p.set_guard(pfx + "sct", sact, ta)
                p.wait("tensor", sact, p.val(sact))
                p.wait_buf("tensor", "gA")
                tmm = p.mm(pA[0:1, 0:TS],
                           [(p.onescb[:, :], es[:, kt, :]) for kt in range(16)],
                           sem_name=smm)
                p.wait("scalar", smm, tmm)
                rr_ = scs[0:1, 0:TS]
                p.wait_buf("scalar", pfx + "scs")
                p.op("scalar", "activation", rr_, pA[0:1, 0:TS], AF.Ln)
                ta = p.op("scalar", "activation", rr_, rr_, AF.Exp, scale=-1.0,
                          sem_name=sact)
                p.set_guard("gA", sact, ta)
                p.wait("tensor", sact, ta)
                p.wait_buf("tensor", "gB")
                tmm = p.mm(pB[:, 0:TS], [(p.ones1f[:, :], rr_)], sem_name=smm)
                p.set_guard(pfx + "scs", smm, tmm)
                p.wait("vector", smm, tmm)
                p.wait_buf("vector", pfx + "rsb")
                trs = p.op("vector", "tensor_copy", rsb[:, :], pB[:, 0:TS],
                           sem_name=sdve)
                p.set_guard("gB", sdve, trs)
                if qb == 0:
                    p.wait("tensor", pfx + "vld", tvh)
                tmm, bank, bk_ = mmg(qb, slice(0, TS),
                                     [(Vh[:, kt, :], es[:, kt, :])
                                      for kt in range(16)])
                if qb == 3:
                    p.set_guard(pfx + "Vh", smm, tmm)
                    p.set_guard(pfx + "es_last", smm, tmm)
                p.wait("vector", smm, tmm)
                p.wait_buf("vector", pfx + "qst_dma")
                td = p.op("vector", "tensor_tensor", qst[:, 0, :],
                          bank[:, 0:TS], rsb[:, :], ALU.mult, sem_name=sdve)
                p.set_guard(bk_, sdve, td)
                p.wait("sync", sdve, td)
                tst = p.dma("sync", at_d[h * 128:(h + 1) * 128,
                                         t0 + qb * TS:t0 + (qb + 1) * TS],
                            qst[:, 0, :], pfx + "qsd")
                p.set_guard(pfx + "qst_dma", pfx + "qsd", tst)
        p.wait("tensor", pfx + "wlg", two)
        for tb in range(4):
            p.wait_buf("sync", pfx + "xt")
            tgt = p.dma("sync", xt[:, :, :],
                        yr_d[:, :, t0 + tb * TS:t0 + (tb + 1) * TS], pfx + "xin")
            p.wait_buf("gpsimd", pfx + "at_ld")
            p.wait_buf("gpsimd", pfx + "qst_dma")
            tat = p.dma("gpsimd", yt[:, :, :],
                        ar[:, :, t0 + tb * TS:t0 + (tb + 1) * TS], pfx + "atl")
            p.wait("tensor", pfx + "atl", tat)
            p.wait("vector", pfx + "xin", tgt)
            p.wait_buf("vector", pfx + "lntscr_g")
            p.wait_buf("vector", pfx + "yf")
            td = None
            for ec in range(CH):
                tmm, bank, bk_ = mmg(ec, slice(0, TS),
                                     [(wt_ap(W["q"], ch, ec), yt[:, ch, :])
                                      for ch in range(CH)])
                p.wait("vector", smm, tmm)
                td = p.op("vector", "tensor_scalar", x1p[:, ec, :],
                          bank[:, 0:TS], bo[:, ec:ec + 1], None, ALU.add)
                td = p.op("vector", "tensor_tensor", x1p[:, ec, :],
                          x1p[:, ec, :], xt[:, ec, :], ALU.add, sem_name=sdve)
                p.set_guard(bk_, sdve, td)
            p.set_guard(pfx + "at_ld", smm, p.val(smm))
            p.set_guard(pfx + "xt", sdve, td)
            p.set_guard(pfx + "yt", smm, p.val(smm))
            ytail = emit_ln_tile(p, ps, x1p[:, :, :], x1f[:, :, :], TS,
                                 gain_col=[g1[:, i:i + 1] for i in range(CH)],
                                 beta_col=[be1[:, i:i + 1] for i in range(CH)])
            p.set_guard(pfx + "lntscr_g", "lndve", ytail)
            p.wait("sync", "lndve", ytail)
            p.wait("gpsimd", "lndve", ytail)
            p.wait_buf("sync", pfx + "x1f_dma")
            t1f = p.dma("sync", x1fr[:, :, t0 + tb * TS:t0 + (tb + 1) * TS],
                        x1f[:, :, :], pfx + "qsd")
            t1b = p.dma("gpsimd", x1br[:, :, t0 + tb * TS:t0 + (tb + 1) * TS],
                        x1f[:, :, :], pfx + "x1b")
            p.set_guard(pfx + "x1f_dma", pfx + "x1b", t1b)
            p.wait("vector", pfx + "qsd", t1f)
        p.set_guard(pfx + "Wq", smm, p.val(smm))
        p.set_guard(pfx + "Wk", smm, p.val(smm))
        p.set_guard(pfx + "Wv", smm, p.val(smm))
    for n in ["x1f", "sct", "scs", "es", "rsb", "nmc", "mrow", "rows", "sqb",
              "sq8", "kqn", "qqh", "qqn", "kq", "qq", "Vh", "KTh", "QTh",
              "vst", "qst", "yt"]:
        p.sfree(pfx + n)
    for n in ["lnrows2", "lnrows1", "lntscr", "lnxb"]:
        p.sfree(pfx + n)
    for n in ["xt", "Wv", "Wk", "Wq"]:
        p.sfree(pfx + n)

    wqo = p.sbuf(pfx + "wqo", [8, FF], BF16)
    p.dma("gpsimd", wqo[:, :], w_in["a1wqo"][:, :], pfx + "wlg")
    w1c = [p.sbuf(pfx + f"w1c{i}", [128, CH, 128], BF16) for i in range(2)]
    w2c = [p.sbuf(pfx + f"w2c{i}", [128, 32, 128], BF16) for i in range(2)]
    h1T = p.sbuf(pfx + "h1T", [128, 32, TS], BF16)
    x1bt = p.sbuf(pfx + "x1bt", [128, CH, TS], BF16)
    x1ft = p.sbuf(pfx + "x1ft", [128, CH, TS], F32)
    qm = p.sbuf(pfx + "qm", [8, TS], F32)
    qt1 = p.sbuf(pfx + "qt1", [8, TS], F32)
    qu = p.sbuf(pfx + "qu", [8, TS], F32)
    qrr = p.sbuf(pfx + "qrr", [8, TS], F32)
    qeb = p.sbuf(pfx + "qeb", [8, TS], BF16)
    x2p = p.sbuf(pfx + "x2p", [128, CH, TS], F32)
    psn = make_ln_scratch_sb(p, TS, pfx + "n")
    psn["psS"], psn["psQ"], psn["psRB"], psn["psBB"] = pA, pB, pC, pD
    psn["gk"] = {"S": "gA", "Q": "gB", "RB": "gC", "BB": "gD",
                 "xb": pfx + "nlnxb_g", "tscr": pfx + "nlntscr_g"}
    o2 = p.sbuf(pfx + "o2", [128, CH, TS], F32)
    x4t = p.sbuf(pfx + "x4t", [128, CH, TS], F32)
    w1r = w_in["a1w1"].rearrange("(ch p) f -> p ch f", p=128)
    w2r = w_in["a1w2"].rearrange("(fc p) e -> p fc e", p=128)

    for k in range(16):
        t0 = k * TS
        p.wait_buf("sync", pfx + "x1bt")
        tb1 = p.dma("sync", x1bt[:, :, :], x1br[:, :, t0:t0 + TS], pfx + "fin")
        p.wait_buf("sync", pfx + "x1ft")
        tf1 = p.dma("sync", x1ft[:, :, :], x1fr[:, :, t0:t0 + TS], pfx + "fin")
        p.wait_buf("sync", pfx + "x4t")
        tx4 = p.dma("sync", x4t[:, :, :], src_r[:, :, t0:t0 + TS], pfx + "fin")
        p.wait("tensor", pfx + "fin", tb1)
        p.wait("tensor", pfx + "wlg", p.val(pfx + "wlg"))
        lastev = None
        for fc in range(32):
            wb = w1c[fc % 2]
            p.wait_buf("gpsimd", pfx + f"w1c{fc % 2}")
            tw = p.dma("gpsimd", wb[:, :, :],
                       w1r[:, :, fc * 128:(fc + 1) * 128], pfx + "fw")
            p.wait("tensor", pfx + "fw", tw)
            tmm, bank, bk_ = mmg(fc, slice(0, TS),
                                 [(wb[:, ch, :], x1bt[:, ch, :])
                                  for ch in range(CH)])
            p.set_guard(pfx + f"w1c{fc % 2}", smm, tmm)
            p.wait("vector", smm, tmm)
            lastev = p.op("vector", "tensor_scalar", h1T[:, fc, :],
                          bank[:, 0:TS], b1c[:, fc:fc + 1], None, ALU.add,
                          sem_name=sdve)
            p.set_guard(bk_, sdve, lastev)
        p.wait("tensor", sdve, lastev)
        p.wait_buf("tensor", "gC")
        tmm = p.mm(pC[0:8, 0:TS],
                   [(wm[:, fc, :], h1T[:, fc, :]) for fc in range(32)],
                   sem_name=smm)
        p.wait("vector", smm, tmm)
        p.wait_buf("vector", pfx + "qm")
        tqm = p.op("vector", "tensor_scalar", qm[:, :], pC[0:8, 0:TS],
                   bmc[:, 0:1], None, ALU.add, sem_name=sdve)
        p.set_guard("gC", sdve, tqm)
        p.wait("scalar", sdve, tqm)
        p.wait_buf("scalar", pfx + "qt1")
        ta1 = p.op("scalar", "activation", qt1[:, :], qm[:, :], AF.Tanh,
                   sem_name=sact)
        p.op("vector", "tensor_scalar", qu[:, :], qm[:, :], 0.125, 0.25,
             ALU.mult, ALU.add)
        p.set_guard(pfx + "qm", sdve if False else sdve, p.val(sdve))
        p.op("vector", "tensor_scalar", qrr[:, :], qu[:, :], MAGIC, MAGIC,
             ALU.add, ALU.subtract)
        tdu = p.op("vector", "tensor_tensor", qu[:, :], qu[:, :], qrr[:, :],
                   ALU.subtract, sem_name=sdve)
        p.wait("scalar", sdve, tdu)
        p.wait_buf("scalar", pfx + "qrr")
        ta2 = p.op("scalar", "activation", qrr[:, :], qu[:, :], AF.Sin,
                   scale=float(2 * math.pi), sem_name=sact)
        p.set_guard(pfx + "qu", sact, ta2)
        p.wait("vector", sact, ta2)
        tqe = p.op("vector", "tensor_tensor", qeb[:, :], qt1[:, :], qrr[:, :],
                   ALU.mult, sem_name=sdve)
        p.set_guard(pfx + "qt1", sdve, tqe)
        p.set_guard(pfx + "qrr", sdve, tqe)
        p.wait("tensor", sdve, tqe)
        lastev = None
        for fc in range(32):
            tmm, bank, bk_ = mmg(fc, slice(0, TS),
                                 [(wqo[:, fc * 128:(fc + 1) * 128], qeb[:, :])])
            if fc == 31:
                p.set_guard(pfx + "qeb", smm, tmm)
            p.wait("vector", smm, tmm)
            lastev = p.op("vector", "tensor_scalar", h1T[:, fc, :],
                          bank[:, 0:TS], bqoc[:, fc:fc + 1], None, ALU.add,
                          sem_name=sdve)
            p.set_guard(bk_, sdve, lastev)
        p.wait("tensor", sdve, lastev)
        p.wait("vector", pfx + "fin", tf1)
        td = None
        for ec in range(CH):
            wb = w2c[ec % 2]
            p.wait_buf("gpsimd", pfx + f"w2c{ec % 2}")
            tw = p.dma("gpsimd", wb[:, :, :],
                       w2r[:, :, ec * 128:(ec + 1) * 128], pfx + "fw")
            p.wait("tensor", pfx + "fw", tw)
            tmm, bank, bk_ = mmg(ec, slice(0, TS),
                                 [(wb[:, fc, :], h1T[:, fc, :])
                                  for fc in range(32)])
            p.set_guard(pfx + f"w2c{ec % 2}", smm, tmm)
            if ec == CH - 1:
                p.set_guard(pfx + "h1T", smm, tmm)
            p.wait("vector", smm, tmm)
            p.op("vector", "tensor_scalar", x2p[:, ec, :], bank[:, 0:TS],
                 b2c[:, ec:ec + 1], None, ALU.add)
            td = p.op("vector", "tensor_tensor", x2p[:, ec, :], x2p[:, ec, :],
                      x1ft[:, ec, :], ALU.add, sem_name=sdve)
            p.set_guard(bk_, sdve, td)
        p.set_guard(pfx + "x1ft", sdve, td)
        p.wait_buf("vector", pfx + "o2")
        ytail = emit_ln_tile(p, psn, x2p[:, :, :], o2[:, :, :], TS,
                             gain_col=[g2[:, i:i + 1] for i in range(CH)],
                             beta_col=[be2[:, i:i + 1] for i in range(CH)])
        p.set_guard(pfx + "x2p_g", "lndve", ytail)
        p.wait("vector", pfx + "fin", tx4)
        tdo = None
        for ch2 in range(CH):
            tdo = p.op("vector", "tensor_tensor", o2[:, ch2, :], o2[:, ch2, :],
                       x4t[:, ch2, :], ALU.add,
                       sem_name=sdve if ch2 == CH - 1 else None)
        p.set_guard(pfx + "x4t", sdve, tdo)
        p.wait("sync", sdve, tdo)
        tso = p.dma("sync", dst_r[:, :, t0:t0 + TS], o2[:, :, :], pfx + "xo")
        p.set_guard(pfx + "o2", pfx + "xo", tso)
        p.set_guard(pfx + "x1bt", smm, p.val(smm))
        p.set_guard(pfx + "qm", sdve, p.val(sdve))
    p.wait("sync", pfx + "xo", p.val(pfx + "xo"))


def make_ln_scratch_sb(p, W, pfx=""):
    return dict(
        xb=p.sbuf(pfx + "lnxb", [128, CH, W], BF16),
        tscr=p.sbuf(pfx + "lntscr", [128, CH, W], F32),
        rows1=p.sbuf(pfx + "lnrows1", [128, W], F32),
        rows2=p.sbuf(pfx + "lnrows2", [128, 2 * W], F32),
    )


def build_a1():
    p = Prog("pa1")
    nc = p.nc
    x_in = nc.dram_tensor("x", [C, TOK], F32, kind="ExternalInput").ap()
    x_out = nc.dram_tensor("xo", [C, TOK], F32, kind="ExternalOutput").ap()
    w_in = {}
    for n, shape in [("a1wq", [C, C]), ("a1wk", [C, C]), ("a1wv", [C, C]),
                     ("a1wo", [C, C]), ("a1wc", [C, NQ]), ("a1w1", [C, FF]),
                     ("a1wm", [FF, NQ]), ("a1wqo", [NQ, FF]), ("a1w2", [FF, C])]:
        w_in[n] = nc.dram_tensor(n, shape, F32, kind="ExternalInput").ap()
    for n, shape in [("a1bq", [128, CH]), ("a1bk", [128, CH]),
                     ("a1bo", [128, CH]), ("a1g1", [128, CH]),
                     ("a1be1", [128, CH]), ("a1g2", [128, CH]),
                     ("a1be2", [128, CH]), ("a1b1", [128, 32]),
                     ("a1bqo", [128, 32]), ("a1b2", [128, CH]),
                     ("a1bc2", [8, 8]), ("a1bm", [8, 1]), ("a1bv", [1, C])]:
        w_in[n] = nc.dram_tensor(n, shape, F32, kind="ExternalInput").ap()
    dram = dict(
        y=nc.dram_tensor("yd", [C, TOK], F32, kind="Internal").ap(),
        qt=nc.dram_tensor("qtd", [C, SL], BF16, kind="Internal").ap(),
        kt=nc.dram_tensor("ktd", [C, SL], BF16, kind="Internal").ap(),
        v=nc.dram_tensor("vd", [SL, C], BF16, kind="Internal").ap(),
        at=nc.dram_tensor("atd", [C, TOK], BF16, kind="Internal").ap(),
        x1f=nc.dram_tensor("x1fd", [C, TOK], F32, kind="Internal").ap(),
        x1b=nc.dram_tensor("x1bd", [C, TOK], BF16, kind="Internal").ap(),
    )
    xr_in = x_in.rearrange("(ch p) t -> p ch t", p=128)
    r_out = x_out.rearrange("(ch p) t -> p ch t", p=128)
    p.init_stage()
    p.stage("a1")
    emit_a1(p, xr_in, r_out, w_in, dram)
    return p.finalize()


def a1_weight_inputs(full_inp):
    g = lambda k: np.asarray(full_inp["L1_attn_" + k], dtype=np.float32)
    d = {}
    d["a1wq"] = np.ascontiguousarray(g("Wq").transpose(1, 0, 2).reshape(C, C))
    d["a1wk"] = np.ascontiguousarray(g("Wk").transpose(1, 0, 2).reshape(C, C))
    d["a1wv"] = np.ascontiguousarray(g("Wv").transpose(1, 0, 2).reshape(C, C))
    d["a1wo"] = np.ascontiguousarray(g("Wo"))
    d["a1wc"] = np.ascontiguousarray(g("Wc2q").transpose(1, 0, 2).reshape(C, NQ))
    d["a1w1"] = np.ascontiguousarray(g("W1"))
    d["a1wm"] = np.ascontiguousarray(g("Wm"))
    d["a1wqo"] = np.ascontiguousarray(g("Wqo"))
    d["a1w2"] = np.ascontiguousarray(g("W2"))
    d["a1bq"] = np.ascontiguousarray(g("bq").T)
    d["a1bk"] = np.ascontiguousarray(g("bk").T)
    d["a1bv"] = np.ascontiguousarray(g("bv").reshape(1, C))
    d["a1bo"] = _packcol(g("bo"))
    d["a1g1"] = _packcol(g("g1"))
    d["a1be1"] = _packcol(g("beta1"))
    d["a1g2"] = _packcol(g("g2"))
    d["a1be2"] = _packcol(g("beta2"))
    d["a1b1"] = _packcol(g("b1"))
    d["a1bqo"] = _packcol(g("bqo"))
    d["a1b2"] = _packcol(g("b2"))
    d["a1bc2"] = np.ascontiguousarray(g("bc2q").T)
    d["a1bm"] = np.ascontiguousarray(g("bm").reshape(NQ, 1))
    return d


def run_a1(x_slices, wd):
    nc = _prog("a1", build_a1)
    in_maps = [dict(x=x_slices[i], **wd) for i in range(NCORES)]
    res = run_bass_kernel_spmd(nc, in_maps, core_ids=list(range(NCORES)))
    return [r["xo"] for r in res.results]



def _featmaj(x3):
    return np.ascontiguousarray(x3.transpose(2, 0, 1).reshape(C, -1))


def _packcol(v):
    n = v.shape[0] // 128
    return np.ascontiguousarray(np.asarray(v, dtype=np.float32).reshape(n, 128).T)


def f1_weight_inputs(full_inp):
    g = lambda k: np.asarray(full_inp["L1_ffn_" + k], dtype=np.float32)
    d = {}
    for n in ["wf", "uf", "wi", "ui", "wc", "uc", "wo", "uo"]:
        d[n] = np.ascontiguousarray(g(n).T)
    for n in ["bf", "bi", "bc", "bo"]:
        d[n] = _packcol(g(n))
    c0 = np.broadcast_to(_packcol(g("in_c"))[:, :, None], (128, CH, PP))
    d["c0"] = np.ascontiguousarray(c0.reshape(128, CH * PP))
    return d


def l0_weight_inputs(full_inp):
    g = lambda k: np.asarray(full_inp[k], dtype=np.float32)
    d = {}
    d["f0r"] = np.ascontiguousarray(g("L0_ffn_W_r").T)
    d["f0k"] = np.ascontiguousarray(g("L0_ffn_W_k").T)
    d["f0v"] = np.ascontiguousarray(g("L0_ffn_W_v").T)
    d["a0q"] = np.ascontiguousarray(g("L0_attn_Wq").T)
    d["a0k"] = np.ascontiguousarray(g("L0_attn_Wk").T)
    d["a0v"] = np.ascontiguousarray(g("L0_attn_Wv").T)
    d["a0o"] = np.ascontiguousarray(g("L0_attn_Wo").T)
    d["a0bq"] = _packcol(g("L0_attn_bq"))
    d["a0bk"] = _packcol(g("L0_attn_bk"))
    d["a0bo"] = _packcol(g("L0_attn_bo"))
    d["a0bv"] = np.ascontiguousarray(g("L0_attn_bv").reshape(1, C))
    d["mur"] = _packcol(g("L0_ffn_mu_r"))
    d["muk"] = _packcol(g("L0_ffn_mu_k"))
    return d


_PROGS = {}


def _prog(name, fn):
    if name not in _PROGS:
        _PROGS[name] = fn()
    return _PROGS[name]


def run_f1(x_slices, wd):
    nc = _prog("f1", build_f1)
    in_maps = [dict(x=x_slices[i], **wd) for i in range(NCORES)]
    res = run_bass_kernel_spmd(nc, in_maps, core_ids=list(range(NCORES)))
    return [r["xo"] for r in res.results]


def run_l0(x_slices, wd):
    nc = _prog("l0", build_l0)
    in_maps = [dict(x=x_slices[i], **wd) for i in range(NCORES)]
    res = run_bass_kernel_spmd(nc, in_maps, core_ids=list(range(NCORES)))
    return [r["xo"] for r in res.results]


def kernel(**inputs):
    x0 = np.ascontiguousarray(np.asarray(inputs["L0_attn_x"], dtype=np.float32))
    l0w = l0_weight_inputs(inputs)
    f1w = f1_weight_inputs(inputs)
    a1w = a1_weight_inputs(inputs)

    def bslices(x_all):
        return [np.ascontiguousarray(
            x_all[:, 4 * i:4 * (i + 1), :].reshape(C, TOK)) for i in range(NCORES)]

    def pslices(x_all):
        return [np.ascontiguousarray(
            x_all[:, :, 256 * j:256 * (j + 1)].reshape(C, TOK))
            for j in range(NCORES)]

    def from_b(outs):
        return np.concatenate([o.reshape(C, 4, SL) for o in outs], axis=1)

    def from_p(outs):
        return np.concatenate([o.reshape(C, NB, PP) for o in outs], axis=2)

    x_all = np.ascontiguousarray(x0.transpose(2, 0, 1))
    x_all = from_b(run_l0(bslices(x_all), l0w))
    x_all = from_p(run_f1(pslices(x_all), f1w))
    x_all = from_b(run_a1(bslices(x_all), a1w))
    x_all = from_p(run_f1(pslices(x_all), f1w))
    return np.ascontiguousarray(x_all.transpose(1, 2, 0))


# version 05 (final)
# speedup vs baseline: 1.1565x; 1.1565x over previous
import math
from contextlib import ExitStack

import numpy as np

import concourse.bass as bass
import concourse.mybir as mybir
from concourse.bass_utils import run_bass_kernel_spmd

F32 = mybir.dt.float32
BF16 = mybir.dt.bfloat16
AF = mybir.ActivationFunctionType
ALU = mybir.AluOpType

MAGIC = float(1.5 * 2 ** 23)
C = 1024
CH = 8
NCORES = 8
TOK = 8192
LN_EPS = 1e-5

ENGINES = ("sync", "gpsimd", "tensor", "vector", "scalar")


class Prog:

    def __init__(self, name, detect_races=True):
        self.nc = bass.Bass("TRN2", target_bir_lowering=False, name=name,
                            detect_race_conditions=detect_races)
        self.stack = ExitStack()
        self.stages = []
        self.sems = {}
        self.cnt = {}
        self._consts = {}
        self._ctile = None
        self._tiles = {}
        self.guard = {}
        self.ones1f = self.sbuf("ones1f", [1, 128], F32)
        self.ones1b = self.sbuf("ones1b", [1, 128], BF16)
        self.onescf = self.sbuf("onescf", [128, 1], F32)
        self.onescb = self.sbuf("onescb", [128, 1], BF16)
        self._ctile = self.sbuf("constpool", [128, 16], F32)

    def sbuf(self, name, shape, dt):
        cm = self.nc.sbuf_tensor(name, shape, dt)
        self._tiles[name] = cm
        return cm.__enter__()

    def sfree(self, name):
        self._tiles.pop(name).__exit__(None, None, None)

    def psum(self, name):
        cm = self.nc.psum_tensor(name, [128, 512], F32)
        self._tiles[name] = cm
        return cm.__enter__()

    def const(self, v):
        v = float(v)
        if v not in self._consts:
            i = len(self._consts)
            assert i < 16
            self._consts[v] = i
            self.nc.const_aps.register(F32, v, self._ctile[:, i:i + 1])
        return self._ctile[:, self._consts[v]:self._consts[v] + 1]

    def const_p(self, v, np_):
        full = self.const(v)
        return self._ctile[0:np_, self._consts[float(v)]:self._consts[float(v)] + 1]

    def sem(self, name):
        if name not in self.sems:
            self.sems[name] = self.stack.enter_context(self.nc.semaphore(name))
            self.cnt[name] = 0
        return self.sems[name]

    def inc(self, name, n=1):
        self.sem(name)
        self.cnt[name] += n
        return self.cnt[name]

    def val(self, name):
        return self.cnt.get(name, 0)

    def stage(self, name):
        self.stages.append((name, {e: [] for e in ENGINES}))

    def on(self, eng, fn):
        self.stages[-1][1][eng].append(fn)

    def wait(self, eng, sem_name, v=None):
        v = self.cnt[sem_name] if v is None else v
        if v <= 0:
            return
        s = self.sem(sem_name)
        self.on(eng, lambda E, s=s, v=v: E.wait_ge(s, v))

    def set_guard(self, buf, sem_name, v=None):
        self.guard[buf] = (sem_name, self.cnt[sem_name] if v is None else v)

    def wait_buf(self, eng, buf):
        g = self.guard.pop(buf, None)
        if g is not None:
            self.wait(eng, g[0], g[1])

    def dma(self, qeng, dst, src, sem_name):
        tgt = self.inc(sem_name, 16)
        s = self.sems[sem_name]
        self.on(qeng, lambda E, d=dst, i=src, s=s: E.dma_start(d, i).then_inc(s, 16))
        return tgt

    def mm(self, psum_ap, pairs, sem_name=None, inc=1):
        s = self.sem(sem_name) if sem_name else None
        tgt = self.inc(sem_name, inc) if sem_name else None

        def run(t, psum_ap=psum_ap, pairs=tuple(pairs), s=s, inc=inc):
            ins = None
            n = len(pairs)
            for i, (l, r) in enumerate(pairs):
                ins = t.matmul(psum_ap, l, r, start=(i == 0), stop=(i == n - 1))
            if s is not None:
                ins.then_inc(s, inc)
        self.on("tensor", run)
        return tgt

    def op(self, eng, fname, *args, sem_name=None, inc=1, **kw):
        s = self.sem(sem_name) if sem_name else None
        if sem_name:
            self.inc(sem_name, inc)

        def run(E, fname=fname, args=args, kw=kw, s=s, inc=inc):
            ins = getattr(E, fname)(*args, **kw)
            if s is not None:
                ins.then_inc(s, inc)
        self.on(eng, run)
        return self.val(sem_name) if sem_name else None

    def init_stage(self):
        self.stage("init")

        def run(v, prog=self):
            v.memset(prog.ones1f[:, :], 1.0)
            v.memset(prog.ones1b[:, :], 1.0)
            v.memset(prog.onescf[:, :], 1.0)
            v.memset(prog.onescb[:, :], 1.0)
            if prog._ctile is not None:
                for val, i in prog._consts.items():
                    v.memset(prog._ctile[:, i:i + 1], val)
        self.on("vector", run)

    def finalize(self):
        nc = self.nc
        for name, ops in self.stages:
            with nc.Block(name) as blk:
                for e in ENGINES:
                    if ops[e]:
                        def runner(E, ops=tuple(ops[e])):
                            for f in ops:
                                f(E)
                        getattr(blk, e)(runner)
        self.stack.close()
        return nc



def emit_ln_tile(p, ps, x_ap, y_tile, W, gain_col=None, beta_col=None):
    xb, tscr = ps["xb"], ps["tscr"]
    rows1, rows2 = ps["rows1"], ps["rows2"]
    r_m = rows1[0:1, 0:W]
    r_t = rows1[32:33, 0:W]
    r_v = rows1[64:65, 0:W]
    r_r = rows2[0:1, 0:W]
    r_b = rows2[0:1, W:2 * W]
    psS, psQ, psRB, psBB = ps["psS"], ps["psQ"], ps["psRB"], ps["psBB"]
    gk = ps.get("gk", {})
    kS = gk.get("S", "ln_psS")
    kQ = gk.get("Q", "ln_psQ")
    kRB = gk.get("RB", "ln_psRB")
    kBB = gk.get("BB", "ln_psBB")
    kxb = gk.get("xb", "ln_xb")
    ktsc = gk.get("tscr", "ln_tscr")

    p.wait_buf("vector", kxb)
    t0 = p.op("vector", "tensor_copy", xb[:, :, :], x_ap, sem_name="lndve")
    p.wait("tensor", "lndve", t0)
    p.wait_buf("tensor", kS)
    tS = p.mm(psS[0:1, 0:W], [(p.onescb[:, :], xb[:, ch, :]) for ch in range(CH)],
              sem_name="lnmm")
    p.wait("scalar", "lnmm", tS)
    t1 = p.op("scalar", "activation", xb[:, :, :], xb[:, :, :], AF.Square,
              sem_name="lnact")
    p.wait("tensor", "lnact", t1)
    p.wait_buf("tensor", kQ)
    tmm = p.mm(psQ[0:1, 0:W], [(p.onescb[:, :], xb[:, ch, :]) for ch in range(CH)],
               sem_name="lnmm")
    p.set_guard(kxb, "lnmm", tmm)
    p.wait("vector", "lnmm", tmm)
    p.op("vector", "tensor_scalar", r_m, psS[0:1, 0:W], 1.0 / C, None, ALU.mult)
    p.op("vector", "tensor_tensor", r_t, r_m, r_m, ALU.mult)
    t2 = p.op("vector", "scalar_tensor_tensor", r_v, psQ[0:1, 0:W], 1.0 / C, r_t,
              ALU.mult, ALU.subtract, sem_name="lndve")
    p.set_guard(kS, "lndve", t2)
    p.set_guard(kQ, "lndve", t2)
    p.wait("scalar", "lndve", t2)
    p.op("scalar", "activation", r_t, r_v, AF.Ln, bias=p.const_p(LN_EPS, 1))
    t3 = p.op("scalar", "activation", r_r, r_t, AF.Exp, scale=-0.5,
              sem_name="lnact")
    p.wait("vector", "lnact", t3)
    t4 = p.op("vector", "scalar_tensor_tensor", r_b, r_m, -1.0, r_r,
              ALU.mult, ALU.mult, sem_name="lndve")
    p.wait("tensor", "lndve", t4)
    p.wait_buf("tensor", kRB)
    p.mm(psRB[:, 0:W], [(p.ones1f[:, :], r_r)])
    p.wait_buf("tensor", kBB)
    tbb = p.mm(psBB[:, 0:W], [(p.ones1f[:, :], r_b)], sem_name="lnmm")
    p.wait("vector", "lnmm", tbb)
    p.wait_buf("vector", ktsc)
    for ch in range(CH):
        p.op("vector", "tensor_tensor", tscr[:, ch, :], x_ap[:, ch, :],
             psRB[:, 0:W], ALU.mult)
    tail = None
    for ch in range(CH):
        if gain_col is None:
            tail = p.op("vector", "tensor_tensor", y_tile[:, ch, :],
                        tscr[:, ch, :], psBB[:, 0:W], ALU.add,
                        sem_name="lndve" if ch == CH - 1 else None)
        else:
            p.op("vector", "tensor_tensor", tscr[:, ch, :], tscr[:, ch, :],
                 psBB[:, 0:W], ALU.add)
            tail = p.op("vector", "tensor_scalar", y_tile[:, ch, :],
                        tscr[:, ch, :], gain_col[ch], beta_col[ch],
                        ALU.mult, ALU.add,
                        sem_name="lndve" if ch == CH - 1 else None)
    p.set_guard(kRB, "lndve", tail)
    p.set_guard(kBB, "lndve", tail)
    return tail


def wt_ap(w, ch, dc):
    return w[:, ch, dc * 128:(dc + 1) * 128]




def make_ln_scratch(p, W, pfx=""):
    return dict(
        xb=p.sbuf(pfx + "lnxb", [128, CH, W], BF16),
        tscr=p.sbuf(pfx + "lntscr", [128, CH, W], F32),
        rows1=p.sbuf(pfx + "lnrows1", [128, W], F32),
        rows2=p.sbuf(pfx + "lnrows2", [128, 2 * W], F32),
        psS=p.psum(pfx + "psS"), psQ=p.psum(pfx + "psQ"),
        psRB=p.psum(pfx + "psRB"), psBB=p.psum(pfx + "psBB"),
    )



NB = 32
PP = 256


def build_f1():
    p = Prog("pf1")
    nc = p.nc
    x_in = nc.dram_tensor("x", [C, TOK], F32, kind="ExternalInput").ap()
    x_out = nc.dram_tensor("xo", [C, TOK], F32, kind="ExternalOutput").ap()
    wnames = ["wf", "uf", "wi", "ui", "wc", "uc"]
    w_in = {n: nc.dram_tensor(n, [C, C], F32, kind="ExternalInput").ap()
            for n in wnames + ["wo", "uo"]}
    b_in = {n: nc.dram_tensor(n, [128, CH], F32, kind="ExternalInput").ap()
            for n in ["bf", "bi", "bc", "bo"]}
    c0_in = nc.dram_tensor("c0", [128, CH * PP], F32, kind="ExternalInput").ap()
    y_dram = nc.dram_tensor("ydram", [C, TOK], BF16, kind="Internal").ap()

    xr = x_in.rearrange("(ch p) t -> p ch t", p=128)
    yr = y_dram.rearrange("(ch p) t -> p ch t", p=128)
    xor_ = x_out.rearrange("(ch p) t -> p ch t", p=128)

    B = {n: p.sbuf("b_" + n, [128, CH], F32) for n in ["bf", "bi", "bc", "bo"]}
    cb = p.sbuf("cb", [128, CH, PP], BF16)
    W = {n: p.sbuf("w_" + n, [128, CH, C], BF16) for n in wnames}

    p.init_stage()
    for n in wnames:
        p.dma("gpsimd", W[n][:, :, :],
              w_in[n].rearrange("(ch p) d -> p ch d", p=128), "wld")
    for n in ["bf", "bi", "bc", "bo"]:
        p.dma("sync", B[n][:, :], b_in[n][:, :], "wldb")

    p.stage("ln")
    TS = 512
    NT = TOK // TS
    xt = [p.sbuf(f"xt{i}", [128, CH, TS], F32) for i in range(2)]
    yt = [p.sbuf(f"yt{i}", [128, CH, TS], BF16) for i in range(2)]
    ps = make_ln_scratch(p, TS)
    for k in range(NT):
        bi = k % 2
        p.wait_buf("sync", f"f1xt{bi}")
        tgt = p.dma("sync", xt[bi][:, :, :], xr[:, :, k * TS:(k + 1) * TS], "xin")
        p.wait("vector", "xin", tgt)
        p.wait_buf("vector", f"f1yt{bi}")
        tail = emit_ln_tile(p, ps, xt[bi][:, :, :], yt[bi][:, :, :], TS)
        p.set_guard(f"f1xt{bi}", "lndve", tail)
        p.wait("sync", "lndve", tail)
        st = p.dma("sync", yr[:, :, k * TS:(k + 1) * TS], yt[bi][:, :, :], "yst")
        p.set_guard(f"f1yt{bi}", "yst", st)

    for n in ["lnrows2", "lnrows1", "lntscr", "lnxb", "yt1", "yt0", "xt1", "xt0",
              "psBB", "psRB", "psQ", "psS"]:
        p.sfree(n)
    p.stage("rec")
    p.wait("sync", "yst", p.val("yst"))
    cbuf = [p.sbuf(f"cst{i}", [128, CH, PP], F32) for i in range(2)]
    ystep = [p.sbuf(f"ys{i}", [128, CH, PP], BF16) for i in range(2)]
    fs = p.sbuf("fs", [128, CH, PP], F32)
    is_ = p.sbuf("is", [128, CH, PP], F32)
    gs = p.sbuf("gs", [128, CH, PP], F32)
    t1 = p.sbuf("t1", [128, CH, PP], F32)
    pg = [p.psum(f"pg{i}") for i in range(6)]

    p.dma("sync", cbuf[0][:, :, :].rearrange("p ch t -> p (ch t)"), c0_in[:, :],
          "cin")
    p.wait("vector", "cin", 16)
    tc = p.op("vector", "tensor_copy", cb[:, :, :], cbuf[0][:, :, :],
              sem_name="cbr")

    gates = [("bf", "wf", "uf", fs, AF.Sigmoid),
             ("bi", "wi", "ui", is_, AF.Sigmoid),
             ("bc", "wc", "uc", gs, AF.Tanh)]
    p.wait("tensor", "wld", p.val("wld"))
    p.wait("tensor", "wldb", p.val("wldb"))
    for b in range(NB):
        cold = cbuf[b % 2]
        cnew = cbuf[(b + 1) % 2]
        ybuf = ystep[b % 2]
        p.wait_buf("sync", f"f1ys{b % 2}")
        tgt = p.dma("sync", ybuf[:, :, :], yr[:, :, b * PP:(b + 1) * PP], "yld")
        p.wait("tensor", "yld", tgt)
        p.wait("tensor", "cbr", p.val("cbr"))
        lastmm = None
        for dc in range(CH):
            for gi, (bn, wn, un, dst, act) in enumerate(gates):
                bank = pg[(dc % 2) * 3 + gi]
                p.wait_buf("tensor", f"pgB{(dc % 2) * 3 + gi}")
                pairs = [(wt_ap(W[wn], ch, dc), ybuf[:, ch, :]) for ch in range(CH)]
                pairs += [(wt_ap(W[un], ch, dc), cb[:, ch, :]) for ch in range(CH)]
                tmm = p.mm(bank[:, 0:PP], pairs, sem_name="gmm")
                lastmm = tmm
                p.wait("scalar", "gmm", tmm)
                tact = p.op("scalar", "activation", dst[:, dc, :], bank[:, 0:PP],
                            act, bias=B[bn][:, dc:dc + 1], sem_name="gact")
                p.set_guard(f"pgB{(dc % 2) * 3 + gi}", "gact", tact)
        p.set_guard(f"f1ys{b % 2}", "gmm", lastmm)
        p.wait("vector", "gact", p.val("gact"))
        p.op("vector", "tensor_tensor", t1[:, :, :], is_[:, :, :], gs[:, :, :],
             ALU.mult)
        p.op("vector", "tensor_tensor", cnew[:, :, :], fs[:, :, :],
             cold[:, :, :], ALU.mult)
        p.op("vector", "tensor_tensor", cnew[:, :, :], cnew[:, :, :],
             t1[:, :, :], ALU.add)
        p.op("vector", "tensor_copy", cb[:, :, :], cnew[:, :, :], sem_name="cbr")

    for n in (["t1", "gs", "is", "fs", "ys1", "ys0", "cst1", "cst0"]
              + ["w_" + n for n in reversed(wnames)]
              + [f"pg{i}" for i in reversed(range(6))]):
        p.sfree(n)
    p.stage("h")
    Wo = p.sbuf("w_wo", [128, CH, C], BF16)
    Uo = p.sbuf("w_uo", [128, CH, C], BF16)
    p.dma("gpsimd", Wo[:, :, :],
          w_in["wo"].rearrange("(ch p) d -> p ch d", p=128), "wld2")
    p.dma("gpsimd", Uo[:, :, :],
          w_in["uo"].rearrange("(ch p) d -> p ch d", p=128), "wld2")
    cu = p.sbuf("cu", [128, CH, PP], F32)
    pcu = [p.psum(f"pcu{i}") for i in range(2)]
    p.wait("tensor", "wld2", p.val("wld2"))
    p.wait("tensor", "cbr", p.val("cbr"))
    for dc in range(CH):
        p.wait_buf("tensor", f"pcuB{dc % 2}")
        tmm = p.mm(pcu[dc % 2][:, 0:PP],
                   [(wt_ap(Uo, ch, dc), cb[:, ch, :]) for ch in range(CH)],
                   sem_name="cumm")
        p.wait("vector", "cumm", tmm)
        tdv = p.op("vector", "tensor_copy", cu[:, dc, :], pcu[dc % 2][:, 0:PP],
                   sem_name="cudve")
        p.set_guard(f"pcuB{dc % 2}", "cudve", tdv)

    hx = [p.sbuf(f"hx{i}", [128, CH, TS], F32) for i in range(2)]
    hy = [p.sbuf(f"hy{i}", [128, CH, TS], BF16) for i in range(2)]
    hp = p.sbuf("hp", [128, CH, TS], F32)
    hh = p.sbuf("hh", [128, CH, TS], F32)
    pho = [p.psum(f"pho{i}") for i in range(2)]
    cud = p.val("cudve")
    for k in range(NT):
        bi = k % 2
        p.wait_buf("sync", f"hy{bi}")
        ty = p.dma("sync", hy[bi][:, :, :], yr[:, :, k * TS:(k + 1) * TS], "hyld")
        p.wait_buf("gpsimd", f"hx{bi}")
        tx = p.dma("gpsimd", hx[bi][:, :, :], xr[:, :, k * TS:(k + 1) * TS],
                   "hxld")
        p.wait("tensor", "hyld", ty)
        lastmm = None
        for dc in range(CH):
            bank = pho[dc % 2]
            p.wait_buf("tensor", f"phoB{dc % 2}")
            tmm = p.mm(bank[:, 0:TS],
                       [(wt_ap(Wo, ch, dc), hy[bi][:, ch, :]) for ch in range(CH)],
                       sem_name="hmm")
            lastmm = tmm
            p.wait("vector", "hmm", tmm)
            if k == 0 and dc == 0:
                p.wait("vector", "cudve", cud)
            p.wait_buf("vector", f"hp{dc}")
            p.op("vector", "tensor_tensor", hp[:, dc, 0:PP], bank[:, 0:PP],
                 cu[:, dc, :], ALU.add)
            tdv = p.op("vector", "tensor_tensor", hp[:, dc, PP:TS],
                       bank[:, PP:TS], cu[:, dc, :], ALU.add, sem_name="hdve")
            p.set_guard(f"phoB{dc % 2}", "hdve", tdv)
            p.wait("scalar", "hdve", tdv)
            p.wait_buf("scalar", f"hh{dc}")
            tact = p.op("scalar", "activation", hh[:, dc, :], hp[:, dc, :],
                        AF.Tanh, bias=B["bo"][:, dc:dc + 1], sem_name="hact")
            p.set_guard(f"hp{dc}", "hact", tact)
        p.set_guard(f"hy{bi}", "hmm", lastmm)
        p.wait("vector", "hact", p.val("hact"))
        p.wait("vector", "hxld", tx)
        tdv = None
        for ch in range(CH):
            tdv = p.op("vector", "scalar_tensor_tensor", hx[bi][:, ch, :],
                       hh[:, ch, :], 0.5, hx[bi][:, ch, :], ALU.mult, ALU.add,
                       sem_name="hodve" if ch == CH - 1 else None)
        for dc in range(CH):
            p.set_guard(f"hh{dc}", "hodve", tdv)
        p.wait("sync", "hodve", tdv)
        st = p.dma("sync", xor_[:, :, k * TS:(k + 1) * TS], hx[bi][:, :, :],
                   "xost")
        p.set_guard(f"hx{bi}", "xost", st)
    p.wait("sync", "xost", p.val("xost"))
    return p.finalize()



SL = 2048


def emit_f0(p, pfx, src_r, dst_r, w_in, mur, omr, muk, omk):
    TS = 512
    NT = TOK // TS
    W = {n: p.sbuf(pfx + "w" + n, [128, CH, C], BF16) for n in ["r", "k", "v"]}
    for n in ["r", "k", "v"]:
        p.dma("gpsimd", W[n][:, :, :],
              w_in["f0" + n].rearrange("(ch p) d -> p ch d", p=128), pfx + "wl")
    xt = [p.sbuf(pfx + f"xt{i}", [128, CH, TS], F32) for i in range(2)]
    yt = [p.sbuf(pfx + f"yt{i}", [128, CH, TS + 1], BF16) for i in range(2)]
    ps = make_ln_scratch(p, TS, pfx)
    xr = p.sbuf(pfx + "xr", [128, CH, TS], BF16)
    xk = p.sbuf(pfx + "xk", [128, CH, TS], BF16)
    rr = p.sbuf(pfx + "rr", [128, CH, TS], F32)
    ka = p.sbuf(pfx + "ka", [128, CH, TS], BF16)
    kr = p.sbuf(pfx + "kr", [128, TS], BF16)
    ot = p.sbuf(pfx + "ot", [128, CH, TS], F32)
    pp = [p.psum(pfx + f"pp{i}") for i in range(2)]
    smm, sdve, sact = pfx + "mm", pfx + "dve", pfx + "act"
    p.wait("tensor", pfx + "wl", p.val(pfx + "wl"))
    for k in range(NT):
        bi = k % 2
        yb, ybp = yt[bi], yt[1 - bi]
        p.wait_buf("sync", pfx + f"xt{bi}")
        tgt = p.dma("sync", xt[bi][:, :, :], src_r[:, :, k * TS:(k + 1) * TS],
                    pfx + "xin")
        p.wait("vector", pfx + "xin", tgt)
        p.wait_buf("vector", pfx + f"yt{bi}")
        if k % 4 == 0:
            p.op("vector", "memset", yb[:, :, 0:1], 0.0)
        else:
            p.op("vector", "tensor_copy", yb[:, :, 0:1], ybp[:, :, TS:TS + 1])
        tail = emit_ln_tile(p, ps, xt[bi][:, :, :], yb[:, :, 1:TS + 1], TS)
        for ch in range(CH):
            p.op("vector", "tensor_scalar", xr[:, ch, :], yb[:, ch, 0:TS],
                 omr[:, ch:ch + 1], None, ALU.mult)
            p.op("vector", "scalar_tensor_tensor", xr[:, ch, :],
                 yb[:, ch, 1:TS + 1], mur[:, ch:ch + 1], xr[:, ch, :],
                 ALU.mult, ALU.add)
            p.op("vector", "tensor_scalar", xk[:, ch, :], yb[:, ch, 0:TS],
                 omk[:, ch:ch + 1], None, ALU.mult)
            t5 = p.op("vector", "scalar_tensor_tensor", xk[:, ch, :],
                      yb[:, ch, 1:TS + 1], muk[:, ch:ch + 1], xk[:, ch, :],
                      ALU.mult, ALU.add,
                      sem_name=sdve if ch == CH - 1 else None)
        p.set_guard(pfx + f"yt{bi}", sdve, t5)
        p.set_guard(pfx + f"xt{bi}_ln", sdve, t5)
        p.wait("tensor", sdve, t5)
        for dc in range(CH):
            bank = pp[dc % 2]
            p.wait_buf("tensor", pfx + f"ppB{dc % 2}")
            tmm = p.mm(bank[:, 0:TS],
                       [(wt_ap(W["r"], ch, dc), xr[:, ch, :]) for ch in range(CH)],
                       sem_name=smm)
            p.wait("scalar", smm, tmm)
            ta = p.op("scalar", "activation", rr[:, dc, :], bank[:, 0:TS],
                      AF.Sigmoid, sem_name=sact)
            p.set_guard(pfx + f"ppB{dc % 2}", sact, ta)
        for dc in range(CH):
            bank = pp[dc % 2]
            p.wait_buf("tensor", pfx + f"ppB{dc % 2}")
            tmm = p.mm(bank[:, 0:TS],
                       [(wt_ap(W["k"], ch, dc), xk[:, ch, :]) for ch in range(CH)],
                       sem_name=smm)
            p.wait("vector", smm, tmm)
            p.wait_buf("vector", pfx + "kr")
            td = p.op("vector", "tensor_scalar", kr[:, :], bank[:, 0:TS], 0.0,
                      None, ALU.max, sem_name=sdve)
            p.wait("scalar", sdve, td)
            ta = p.op("scalar", "activation", ka[:, dc, :], kr[:, :], AF.Square,
                      sem_name=sact)
            p.set_guard(pfx + "kr", sact, ta)
            p.set_guard(pfx + f"ppB{dc % 2}", sact, ta)
        p.wait("tensor", sact, p.val(sact))
        for dc in range(CH):
            bank = pp[dc % 2]
            p.wait_buf("tensor", pfx + f"ppB{dc % 2}")
            tmm = p.mm(bank[:, 0:TS],
                       [(wt_ap(W["v"], ch, dc), ka[:, ch, :]) for ch in range(CH)],
                       sem_name=smm)
            p.wait("vector", smm, tmm)
            p.op("vector", "scalar_tensor_tensor", ot[:, dc, :], bank[:, 0:TS],
                 0.5, rr[:, dc, :], ALU.mult, ALU.mult)
            td = p.op("vector", "tensor_tensor", ot[:, dc, :], ot[:, dc, :],
                      xt[bi][:, dc, :], ALU.add, sem_name=sdve)
            p.set_guard(pfx + f"ppB{dc % 2}", sdve, td)
        p.set_guard(pfx + f"xt{bi}", sdve, td)
        p.wait("sync", sdve, td)
        st = p.dma("sync", dst_r[:, :, k * TS:(k + 1) * TS], ot[:, :, :],
                   pfx + "xo")
        p.set_guard(pfx + "ot", pfx + "xo", st)
        p.wait_buf("vector", pfx + "ot")
    p.wait("sync", pfx + "xo", p.val(pfx + "xo"))
    for n in [pfx + "pp1", pfx + "pp0"]:
        p.sfree(n)
    for n in ["ot", "kr", "ka", "rr", "xk", "xr"]:
        p.sfree(pfx + n)
    for n in ["psBB", "psRB", "psQ", "psS", "lnrows2", "lnrows1", "lntscr", "lnxb"]:
        p.sfree(pfx + n)
    for n in ["yt1", "yt0", "xt1", "xt0", "wv", "wk", "wr"]:
        p.sfree(pfx + n)


def emit_a0(p, src_r, dst_r, w_in, qt_dram, at_dram, upto=9, dbg=None):
    TS = 512
    pfx = "a"
    W = {n: p.sbuf(pfx + "W" + n, [128, CH, C], BF16) for n in ["q", "k", "v"]}
    bq = p.sbuf(pfx + "bq", [128, CH], F32)
    bk = p.sbuf(pfx + "bk", [128, CH], F32)
    bo = p.sbuf(pfx + "bo", [128, CH], F32)
    bvr = p.sbuf(pfx + "bvr", [1, C], BF16)
    p.dma("sync", bq[:, :], w_in["a0bq"][:, :], pfx + "wl")
    p.dma("sync", bk[:, :], w_in["a0bk"][:, :], pfx + "wl")
    p.dma("sync", bo[:, :], w_in["a0bo"][:, :], pfx + "wl")
    p.dma("gpsimd", bvr[:, :], w_in["a0bv"][:, :], pfx + "wlg")

    KT = p.sbuf(pfx + "KT", [128, CH, SL], BF16)
    Vs = p.sbuf(pfx + "Vs", [128, 16, C], BF16)
    xt = p.sbuf(pfx + "xt", [128, CH, TS], F32)
    ps = make_ln_scratch(p, TS, pfx)
    yt = p.sbuf(pfx + "yt", [128, CH, TS], BF16)
    rows = p.sbuf(pfx + "rows", [128, SL], F32)
    knc = p.sbuf(pfx + "knc", [128, 16], F32)
    nmc = p.sbuf(pfx + "nmc", [128, 1], F32)
    rqb = p.sbuf(pfx + "rqb", [128, TS], F32)
    rsb = rqb
    es = p.sbuf(pfx + "es", [128, 16, TS], BF16)
    qst = es[:, 0:CH, :]
    sc1d = p.sbuf(pfx + "sc1", [128, 1, TS], F32)
    sc2d = p.sbuf(pfx + "sc2", [128, 1, TS], F32)
    sc1 = sc1d[:, 0, :]
    sc2 = sc2d[:, 0, :]
    s1p = [sc1, xt[:, 0, :]]
    s2p = [sc2, xt[:, 1, :]]
    mrow = p.sbuf(pfx + "mrow", [1, 8], F32)
    at = p.sbuf(pfx + "at", [128, CH, TS], BF16)
    pj = [p.psum(pfx + f"pj{i}") for i in range(2)]
    prow = p.psum(pfx + "prow")
    pcol = p.psum(pfx + "pcol")
    smm, sdve, sact = pfx + "mm", pfx + "dve", pfx + "act"
    qr = qt_dram.rearrange("(ch p) t -> p ch t", p=128)
    ar = at_dram.rearrange("(ch p) t -> p ch t", p=128)
    ot = ps["tscr"]
    Qq = yt
    sqs = at
    rQ = rows[0:1, :]
    rK = rows[32:33, :]
    rr_ = sc1[0:1, 0:TS]
    mx = mrow[0:1, 0:8]

    p.wait("tensor", pfx + "wl", p.val(pfx + "wl"))
    for b in (range(1) if (dbg and dbg != "y") else range(4)):
        t0 = b * SL
        for n, key in [("q", "a0q"), ("k", "a0k"), ("v", "a0v")]:
            p.wait_buf("gpsimd", pfx + "W" + n)
            p.dma("gpsimd", W[n][:, :, :],
                  w_in[key].rearrange("(ch p) d -> p ch d", p=128), pfx + "wlg")
        twl = p.val(pfx + "wlg")
        p.wait("tensor", pfx + "wlg", twl)
        for tb in range(4):
            p.wait_buf("sync", pfx + "xt")
            tgt = p.dma("sync", xt[:, :, :],
                        src_r[:, :, t0 + tb * TS:t0 + (tb + 1) * TS], pfx + "xin")
            p.wait("vector", pfx + "xin", tgt)
            p.wait_buf("vector", pfx + "yt")
            ytail = emit_ln_tile(p, ps, xt[:, :, :], yt[:, :, :], TS)
            p.set_guard(pfx + "xt", "lndve", ytail)
            p.wait("tensor", "lndve", ytail)
            if dbg == "y":
                p.wait("sync", "lndve", ytail)
                p.wait_buf("sync", pfx + "ydbg")
                tq = p.dma("sync", qr[:, :, t0 + tb * TS:t0 + (tb + 1) * TS],
                           yt[:, :, :], pfx + "qst")
                p.set_guard(pfx + "ydbg", pfx + "qst", tq)
                p.set_guard(pfx + "yt", pfx + "qst", tq)
                continue
            p.wait_buf("tensor", pfx + "qst_dma")
            p.wait_buf("scalar", pfx + "at_sq")
            for dc in range(CH):
                bank = pj[dc % 2]
                p.wait_buf("tensor", pfx + f"pjB{dc % 2}")
                tmm = p.mm(bank[:, 0:TS],
                           [(wt_ap(W["q"], ch, dc), yt[:, ch, :])
                            for ch in range(CH)], sem_name=smm)
                if dc == CH - 1:
                    p.set_guard(pfx + "Wq_inuse", smm, tmm)
                p.wait("vector", smm, tmm)
                td = p.op("vector", "tensor_scalar", qst[:, dc, :],
                          bank[:, 0:TS], bq[:, dc:dc + 1], None, ALU.add,
                          sem_name=sdve)
                p.set_guard(pfx + f"pjB{dc % 2}", sdve, td)
                p.wait("scalar", sdve, td)
                ta = p.op("scalar", "activation", sqs[:, dc, :], qst[:, dc, :],
                          AF.Square, sem_name=sact)
            p.wait("tensor", sact, p.val(sact))
            p.wait_buf("tensor", pfx + "prow")
            tmm = p.mm(prow[0:1, 0:TS],
                       [(p.onescb[:, :], sqs[:, ch, :]) for ch in range(CH)],
                       sem_name=smm)
            p.wait("vector", smm, tmm)
            trow = p.op("vector", "tensor_copy", rQ[0:1, tb * TS:(tb + 1) * TS],
                        prow[0:1, 0:TS], sem_name=sdve)
            p.set_guard(pfx + "prow", sdve, trow)
            p.wait("sync", sdve, trow)
            tq = p.dma("sync", qr[:, :, t0 + tb * TS:t0 + (tb + 1) * TS],
                       qst[:, :, :], pfx + "qst")
            p.set_guard(pfx + "qst_dma", pfx + "qst", tq)
            if tb == 3:
                p.set_guard(pfx + "esq", pfx + "qst", tq)
            for dc in range(CH):
                bank = pj[dc % 2]
                p.wait_buf("tensor", pfx + f"pjB{dc % 2}")
                tmm = p.mm(bank[:, 0:TS],
                           [(wt_ap(W["k"], ch, dc), yt[:, ch, :])
                            for ch in range(CH)], sem_name=smm)
                p.wait("vector", smm, tmm)
                td = p.op("vector", "tensor_scalar",
                          KT[:, dc, tb * TS:(tb + 1) * TS],
                          bank[:, 0:TS], bk[:, dc:dc + 1], None, ALU.add,
                          sem_name=sdve)
                p.set_guard(pfx + f"pjB{dc % 2}", sdve, td)
                p.wait("scalar", sdve, td)
                ta = p.op("scalar", "activation", sqs[:, dc, :],
                          KT[:, dc, tb * TS:(tb + 1) * TS],
                          AF.Square, sem_name=sact)
            p.wait("tensor", sact, p.val(sact))
            p.wait_buf("tensor", pfx + "prow")
            tmm = p.mm(prow[0:1, 0:TS],
                       [(p.onescb[:, :], sqs[:, ch, :]) for ch in range(CH)],
                       sem_name=smm)
            p.wait("vector", smm, tmm)
            trow = p.op("vector", "tensor_copy", rK[0:1, tb * TS:(tb + 1) * TS],
                        prow[0:1, 0:TS], sem_name=sdve)
            p.set_guard(pfx + "prow", sdve, trow)
            for tc in range(4):
                p.wait_buf("tensor", pfx + "pcol")
                tmm = p.mm(pcol[:, 0:1],
                           [(sqs[:, ch, tc * 128:(tc + 1) * 128], p.onescb[:, :])
                            for ch in range(CH)], sem_name=smm)
                p.wait("vector", smm, tmm)
                tkc = p.op("vector", "tensor_copy",
                           knc[:, tb * 4 + tc:tb * 4 + tc + 1],
                           pcol[:, 0:1], sem_name=sdve)
                p.set_guard(pfx + "pcol", sdve, tkc)
            tsq_last = p.val(smm)
            p.set_guard(pfx + "at_sq", smm, tsq_last)
            for tc in range(4):
                for dh in range(2):
                    bank = pj[dh]
                    p.wait_buf("tensor", pfx + f"pjB{dh}")
                    pairs = [(yt[:, ch, tc * 128:(tc + 1) * 128],
                              W["v"][:, ch, dh * TS:(dh + 1) * TS])
                             for ch in range(CH)]
                    pairs.append((p.ones1b[:, :], bvr[0:1, dh * TS:(dh + 1) * TS]))
                    tmm = p.mm(bank[:, 0:TS], pairs, sem_name=smm)
                    p.wait("vector", smm, tmm)
                    td = p.op("vector", "tensor_copy",
                              Vs[:, tb * 4 + tc, dh * TS:(dh + 1) * TS],
                              bank[:, 0:TS], sem_name=sdve)
                    p.set_guard(pfx + f"pjB{dh}", sdve, td)
        if dbg == "y":
            continue
        if dbg == "k":
            p.wait("gpsimd", sdve, p.val(sdve))
            p.dma("gpsimd",
                  qt_dram.rearrange("(p a) t -> p a t", p=128)[:, 5, 0:1024],
                  KT[:, :, 1920:2048], pfx + "edmp")
            p.wait("gpsimd", pfx + "edmp", p.val(pfx + "edmp"))
            continue
        p.set_guard(pfx + "yt", smm, p.val(smm))
        if upto < 2:
            continue
        p.wait_buf("gpsimd", pfx + "Wq_inuse")
        p.wait_buf("gpsimd", pfx + "Wq")
        p.dma("gpsimd", W["q"][:, :, :],
              w_in["a0o"].rearrange("(ch p) d -> p ch d", p=128), pfx + "wlg")
        two = p.val(pfx + "wlg")
        p.wait("scalar", sdve, p.val(sdve))
        p.op("scalar", "activation", rQ, rQ, AF.Sqrt)
        ta = p.op("scalar", "activation", rK, rK, AF.Sqrt, sem_name=sact)
        p.wait("vector", sact, ta)
        p.op("vector", "reduce_max", mx[0:1, 0:1], rQ, mybir.AxisListType.X)
        tv = p.op("vector", "reduce_max", mx[0:1, 1:2], rK, mybir.AxisListType.X,
                  sem_name=sdve)
        p.wait("scalar", sdve, tv)
        p.op("scalar", "activation", rQ, rQ, AF.Ln, bias=p.const_p(1e-8, 1))
        ta = p.op("scalar", "activation", rQ, rQ, AF.Exp, scale=-1.0,
                  sem_name=sact)
        p.wait("vector", sact, ta)
        tv = p.op("vector", "scalar_tensor_tensor", mx[0:1, 2:3], mx[0:1, 0:1],
                  -1.0 / 32.0, mx[0:1, 1:2], ALU.mult, ALU.mult, sem_name=sdve)
        p.wait("tensor", sdve, tv)
        p.wait_buf("tensor", pfx + "pcol")
        tmm = p.mm(pcol[:, 0:1], [(p.ones1f[:, :], mx[0:1, 2:3])], sem_name=smm)
        p.wait("vector", smm, tmm)
        tnm = p.op("vector", "tensor_copy", nmc[:, :], pcol[:, 0:1],
                   sem_name=sdve)
        p.set_guard(pfx + "pcol", sdve, tnm)
        p.wait("scalar", sdve, p.val(sdve))
        p.op("scalar", "activation", knc[:, :], knc[:, :], AF.Sqrt)
        p.op("scalar", "activation", knc[:, :], knc[:, :], AF.Ln,
             bias=p.const(1e-8))
        tk = p.op("scalar", "activation", knc[:, :], knc[:, :], AF.Exp,
                  scale=-1.0, sem_name=sact)
        if upto < 3:
            continue
        if dbg == "k2":
            p.wait("gpsimd", sact, p.val(sact))
            p.wait("gpsimd", sdve, p.val(sdve))
            p.dma("gpsimd",
                  qt_dram.rearrange("(p a) t -> p a t", p=128)[:, 5, 0:1024],
                  KT[:, :, 1920:2048], pfx + "edmp")
            p.wait("gpsimd", pfx + "edmp", p.val(pfx + "edmp"))
            continue
        p.wait("tensor", pfx + "wlg", two)
        p.wait_buf("vector", pfx + "xt")
        for qb in (range(1) if (dbg and dbg not in ("y", "k", "k2")) else range(4)):
            p.wait_buf("sync", pfx + "yt")
            p.wait_buf("sync", pfx + "Qq")
            qtg = p.dma("sync", Qq[:, :, :],
                        qr[:, :, t0 + qb * TS:t0 + (qb + 1) * TS], pfx + "qld")
            p.wait("tensor", sact, p.val(sact))
            p.wait("tensor", sdve, p.val(sdve))
            p.wait_buf("tensor", pfx + "pcol")
            tmm = p.mm(pcol[:, 0:TS],
                       [(p.ones1f[:, :], rQ[0:1, qb * TS:(qb + 1) * TS])],
                       sem_name=smm)
            p.wait("vector", smm, tmm)
            p.wait_buf("vector", pfx + "rqb")
            trq = p.op("vector", "tensor_copy", rqb[:, :], pcol[:, 0:TS],
                       sem_name=sdve)
            p.set_guard(pfx + "pcol", sdve, trq)
            if dbg == "e":
                p.wait("gpsimd", sdve, trq)
                p.dma("gpsimd",
                      qt_dram.rearrange("(p a) t -> p a t", p=128)[:, 2, 0:TS],
                      rqb[:, :], pfx + "edmp")
                p.wait("gpsimd", pfx + "act", p.val(pfx + "act"))
                p.dma("gpsimd",
                      qt_dram.rearrange("(p a) t -> p a t", p=128)[:, 3, 0:16],
                      knc[:, :], pfx + "edmp")
            p.wait("tensor", pfx + "qld", qtg)
            if qb == 0:
                p.wait_buf("scalar", pfx + "esq")
            pend = []
            grp = []
            for kt in range(16):
                par = kt % 2
                bank = pj[par]
                p.wait_buf("tensor", pfx + f"pjB{par}")
                tmm = p.mm(bank[:, 0:TS],
                           [(KT[:, ch, kt * 128:(kt + 1) * 128], Qq[:, ch, :])
                            for ch in range(CH)], sem_name=smm)
                if kt == 15:
                    p.set_guard(pfx + "Qq", smm, tmm)
                grp.append((kt, par, tmm, bank))
                if par == 0:
                    continue
                for (kp, pp, sctp, tdp) in pend:
                    p.wait("scalar", sdve, tdp)
                    ta = p.op("scalar", "activation", es[:, kp, :], sctp,
                              AF.Exp, bias=nmc[:, 0:1], sem_name=sact)
                    p.set_guard(pfx + f"scP{pp}", sact, ta)
                pend = []
                td1 = {}
                for (k_, p_, tm_, bk_) in grp:
                    s1 = s1p[p_]
                    s2 = s2p[p_]
                    p.wait("vector", smm, tm_)
                    p.wait_buf("vector", pfx + f"scP{p_}")
                    p.op("vector", "tensor_scalar", s1, rqb[:, :],
                         knc[:, k_:k_ + 1], -0.5, ALU.mult, ALU.mult)
                    p.op("vector", "tensor_tensor", s1, bk_[:, 0:TS],
                         s1, ALU.mult)
                    p.op("vector", "tensor_scalar", s2, s1, 0.25,
                         None, ALU.add)
                    p.op("vector", "tensor_scalar", s1, s2, MAGIC,
                         MAGIC, ALU.add, ALU.subtract)
                    td1[k_] = p.op("vector", "tensor_tensor", s2, s2,
                                   s1, ALU.subtract, sem_name=sdve)
                tsin = {}
                for (k_, p_, tm_, bk_) in grp:
                    p.wait("scalar", sdve, td1[k_])
                    tsin[k_] = p.op("scalar", "activation", s1p[p_],
                                    s2p[p_], AF.Sin,
                                    scale=float(2 * math.pi), sem_name=sact)
                for (k_, p_, tm_, bk_) in grp:
                    p.wait("vector", sact, tsin[k_])
                    td2 = p.op("vector", "scalar_tensor_tensor",
                               s2p[p_], bk_[:, 0:TS], 1.0 / 32.0,
                               s1p[p_], ALU.mult, ALU.mult,
                               sem_name=sdve)
                    p.set_guard(pfx + f"pjB{p_}", sdve, td2)
                    pend.append((k_, p_, s2p[p_], td2))
                grp = []
            for (kp, pp, sctp, tdp) in pend:
                p.wait("scalar", sdve, tdp)
                ta = p.op("scalar", "activation", es[:, kp, :], sctp,
                          AF.Exp, bias=nmc[:, 0:1], sem_name=sact)
                p.set_guard(pfx + f"scP{pp}", sact, ta)
                p.set_guard(pfx + "xt", sact, ta)
            pend = []
            if dbg in ("k3", "k4", "k5", "k6", "k6a", "k6b") and qb == 0:
                p.wait("gpsimd", smm, p.val(smm))
                p.wait("gpsimd", sdve, p.val(sdve))
                p.wait("gpsimd", sact, p.val(sact))
                p.dma("gpsimd",
                      qt_dram.rearrange("(p a) (c k) -> p a c k", p=128, c=4)
                      [:, 4, :, :],
                      KT[:, 0:4, :], pfx + "edmp")
                p.dma("gpsimd",
                      qt_dram.rearrange("(p a) (c k) -> p a c k", p=128, c=4)
                      [:, 5, :, :],
                      KT[:, 4:8, :], pfx + "edmp")
                p.dma("gpsimd",
                      qt_dram.rearrange("(p a) (c k) -> p a c k", p=128, c=8)
                      [:, 6, :, :],
                      Vs[:, 0:8, :], pfx + "edmp")
                p.dma("gpsimd",
                      qt_dram.rearrange("(p a) (c k) -> p a c k", p=128, c=8)
                      [:, 7, :, :],
                      Vs[:, 8:16, :], pfx + "edmp")
                p.wait("gpsimd", pfx + "edmp", p.val(pfx + "edmp"))
            if dbg in ("e", "e2") and b == 0 and qb == 0:
                p.wait("sync", sact, p.val(sact))
                p.dma("sync", qt_dram.rearrange("(p a) t -> p a t", p=128)[:, 0, 0:16 * TS],
                      es[:, :, :].rearrange("p k t -> p (k t)"), pfx + "edmp")
                p.dma("sync", qt_dram.rearrange("(p a) t -> p a t", p=128)[:, 2, 0:16 * TS],
                      es[:, :, :].rearrange("p k t -> p (k t)"), pfx + "edmp")
                p.wait("sync", pfx + "edmp", p.val(pfx + "edmp"))
            if upto < 4:
                continue
            p.wait("tensor", sact, p.val(sact))
            p.wait_buf("tensor", pfx + "prow")
            tmm = p.mm(prow[0:1, 0:TS],
                       [(p.onescb[:, :], es[:, kt, :]) for kt in range(16)],
                       sem_name=smm)
            p.wait("scalar", smm, tmm)
            p.op("scalar", "activation", rr_, prow[0:1, 0:TS], AF.Ln)
            ta = p.op("scalar", "activation", rr_, rr_, AF.Exp, scale=-1.0,
                      sem_name=sact)
            p.set_guard(pfx + "prow", sact, ta)
            p.wait("tensor", sact, ta)
            p.wait_buf("tensor", pfx + "pcol")
            tmm = p.mm(pcol[:, 0:TS], [(p.ones1f[:, :], rr_)], sem_name=smm)
            p.wait("vector", smm, tmm)
            p.wait_buf("vector", pfx + "rsb")
            trs = p.op("vector", "tensor_copy", rsb[:, :], pcol[:, 0:TS],
                       sem_name=sdve)
            p.set_guard(pfx + "pcol", sdve, trs)
            if upto < 5:
                continue
            p.wait_buf("vector", pfx + "at_dma")
            for dc in range(CH):
                bank = pj[dc % 2]
                p.wait_buf("tensor", pfx + f"pjB{dc % 2}")
                tmm = p.mm(bank[:, 0:TS],
                           [(Vs[:, kt, dc * 128:(dc + 1) * 128], es[:, kt, :])
                            for kt in range(16)], sem_name=smm)
                p.wait("vector", smm, tmm)
                td = p.op("vector", "tensor_tensor", at[:, dc, :], bank[:, 0:TS],
                          rsb[:, :], ALU.mult, sem_name=sdve)
                p.set_guard(pfx + f"pjB{dc % 2}", sdve, td)
            p.wait("sync", sdve, td)
            tst = p.dma("sync", ar[:, :, t0 + qb * TS:t0 + (qb + 1) * TS],
                        at[:, :, :], pfx + "ast")
            p.set_guard(pfx + "at_dma", pfx + "ast", tst)
        if upto < 6:
            continue
        for tb in range(4):
            p.wait_buf("sync", pfx + "xt")
            tgt = p.dma("sync", xt[:, :, :],
                        src_r[:, :, t0 + tb * TS:t0 + (tb + 1) * TS], pfx + "xin")
            p.wait_buf("gpsimd", pfx + "at_ld")
            p.wait_buf("gpsimd", pfx + "at_dma")
            tat = p.dma("gpsimd", at[:, :, :],
                        ar[:, :, t0 + tb * TS:t0 + (tb + 1) * TS], pfx + "atl")
            p.wait("tensor", pfx + "atl", tat)
            p.wait("vector", pfx + "xin", tgt)
            p.wait_buf("vector", "ln_tscr")
            for ec in range(CH):
                bank = pj[ec % 2]
                p.wait_buf("tensor", pfx + f"pjB{ec % 2}")
                tmm = p.mm(bank[:, 0:TS],
                           [(wt_ap(W["q"], ch, ec), at[:, ch, :])
                            for ch in range(CH)], sem_name=smm)
                p.wait("vector", smm, tmm)
                p.op("vector", "tensor_scalar", ot[:, ec, :], bank[:, 0:TS],
                     bo[:, ec:ec + 1], None, ALU.add)
                td = p.op("vector", "tensor_tensor", ot[:, ec, :], ot[:, ec, :],
                          xt[:, ec, :], ALU.add, sem_name=sdve)
                p.set_guard(pfx + f"pjB{ec % 2}", sdve, td)
            p.set_guard(pfx + "xt", sdve, td)
            p.set_guard(pfx + "at_ld", smm, p.val(smm))
            p.wait("sync", sdve, td)
            tso = p.dma("sync", dst_r[:, :, t0 + tb * TS:t0 + (tb + 1) * TS],
                        ot[:, :, :], pfx + "xo")
            p.set_guard("ln_tscr", pfx + "xo", tso)
            p.wait_buf("vector", "ln_tscr_next")
        p.set_guard(pfx + "Wq", smm, p.val(smm))
        p.set_guard(pfx + "Wk", smm, p.val(smm))
        p.set_guard(pfx + "Wv", smm, p.val(smm))
    p.wait("sync", pfx + "xo", p.val(pfx + "xo"))
    for n in ["pcol", "prow", "pj1", "pj0"]:
        p.sfree(n if n.startswith(pfx) else pfx + n)
    for n in ["at", "mrow", "sc2", "sc1", "es", "rqb", "nmc", "knc",
              "rows", "yt"]:
        p.sfree(pfx + n)
    for n in ["lnrows2", "lnrows1", "lntscr", "lnxb"]:
        p.sfree(pfx + n)
    for n in ["psBB", "psRB", "psQ", "psS"]:
        p.sfree(pfx + n)
    for n in ["xt", "Vs", "KT", "bvr", "bo", "bk", "bq", "Wv", "Wk", "Wq"]:
        p.sfree(pfx + n)



def build_l0():
    p = Prog("pl0")
    nc = p.nc
    x_in = nc.dram_tensor("x", [C, TOK], F32, kind="ExternalInput").ap()
    x_out = nc.dram_tensor("xo", [C, TOK], F32, kind="ExternalOutput").ap()
    names = (["f0r", "f0k", "f0v", "a0q", "a0k", "a0v", "a0o"])
    w_in = {n: nc.dram_tensor(n, [C, C], F32, kind="ExternalInput").ap()
            for n in names}
    for n in ["a0bq", "a0bk", "a0bo"]:
        w_in[n] = nc.dram_tensor(n, [128, CH], F32, kind="ExternalInput").ap()
    w_in["a0bv"] = nc.dram_tensor("a0bv", [1, C], F32, kind="ExternalInput").ap()
    for n in ["mur", "muk"]:
        w_in[n] = nc.dram_tensor(n, [128, CH], F32, kind="ExternalInput").ap()
    xb1 = nc.dram_tensor("xb1", [C, TOK], F32, kind="Internal").ap()
    xb2 = nc.dram_tensor("xb2", [C, TOK], F32, kind="Internal").ap()
    qt_dram = nc.dram_tensor("qtd", [C, TOK], BF16, kind="Internal").ap()
    at_dram = nc.dram_tensor("atd", [C, TOK], BF16, kind="Internal").ap()

    xr_in = x_in.rearrange("(ch p) t -> p ch t", p=128)
    r1 = xb1.rearrange("(ch p) t -> p ch t", p=128)
    r2 = xb2.rearrange("(ch p) t -> p ch t", p=128)
    r_out = x_out.rearrange("(ch p) t -> p ch t", p=128)

    mur = p.sbuf("s_mur", [128, CH], F32)
    omr = p.sbuf("s_omr", [128, CH], F32)
    muk = p.sbuf("s_muk", [128, CH], F32)
    omk = p.sbuf("s_omk", [128, CH], F32)

    p.init_stage()
    p.dma("sync", mur[:, :], w_in["mur"][:, :], "mul")
    p.dma("sync", muk[:, :], w_in["muk"][:, :], "mul")
    p.wait("vector", "mul", p.val("mul"))
    p.op("vector", "tensor_scalar", omr[:, :], mur[:, :], -1.0, 1.0,
         ALU.mult, ALU.add)
    p.op("vector", "tensor_scalar", omk[:, :], muk[:, :], -1.0, 1.0,
         ALU.mult, ALU.add)

    p.stage("f0a")
    emit_f0(p, "fa", xr_in, r1, w_in, mur, omr, muk, omk)
    p.stage("a0")
    emit_a0(p, r1, r2, w_in, qt_dram, at_dram)
    p.stage("f0b")
    emit_f0(p, "fb", r2, r_out, w_in, mur, omr, muk, omk)
    return p.finalize()





FF = 4096
NQ = 8
C2 = float(0.5 / math.sqrt(128.0))


def emit_a1(p, src_r, dst_r, w_in, dram):
    TS = 512
    pfx = "m"
    smm, sdve, sact = pfx + "mm", pfx + "dve", pfx + "act"
    qt_d, kt_d, v_d, at_d, x1f_d, x1b_d, y_d = (
        dram["qt"], dram["kt"], dram["v"], dram["at"], dram["x1f"],
        dram["x1b"], dram["y"])
    yr_d = y_d.rearrange("(ch p) t -> p ch t", p=128)
    ar = at_d.rearrange("(ch p) t -> p ch t", p=128)
    x1fr = x1f_d.rearrange("(ch p) t -> p ch t", p=128)
    x1br = x1b_d.rearrange("(ch p) t -> p ch t", p=128)

    bq = p.sbuf(pfx + "bq", [128, CH], F32)
    bk = p.sbuf(pfx + "bk", [128, CH], F32)
    bk2 = p.sbuf(pfx + "bk2", [128, CH], F32)
    bo = p.sbuf(pfx + "bo", [128, CH], F32)
    g1 = p.sbuf(pfx + "g1", [128, CH], F32)
    be1 = p.sbuf(pfx + "be1", [128, CH], F32)
    g2 = p.sbuf(pfx + "g2", [128, CH], F32)
    be2 = p.sbuf(pfx + "be2", [128, CH], F32)
    b1c = p.sbuf(pfx + "b1c", [128, 32], F32)
    bqoc = p.sbuf(pfx + "bqoc", [128, 32], F32)
    b2c = p.sbuf(pfx + "b2c", [128, CH], F32)
    bc2 = p.sbuf(pfx + "bc2", [8, 8], F32)
    bmc = p.sbuf(pfx + "bmc", [8, 1], F32)
    bvr = p.sbuf(pfx + "bvr", [1, C], BF16)
    wc = p.sbuf(pfx + "wc", [128, CH, NQ], BF16)
    wm = p.sbuf(pfx + "wm", [128, 32, NQ], BF16)
    for n, t in [("a1bq", bq), ("a1bk", bk), ("a1bo", bo), ("a1g1", g1),
                 ("a1be1", be1), ("a1g2", g2), ("a1be2", be2), ("a1b1", b1c),
                 ("a1bqo", bqoc), ("a1b2", b2c), ("a1bc2", bc2), ("a1bm", bmc)]:
        p.dma("sync", t[:, :], w_in[n][:, :], pfx + "wl")
    p.dma("gpsimd", bvr[:, :], w_in["a1bv"][:, :], pfx + "wlg")
    p.dma("gpsimd", wc[:, :, :],
          w_in["a1wc"].rearrange("(h p) n -> p h n", p=128), pfx + "wlg")
    p.dma("gpsimd", wm[:, :, :],
          w_in["a1wm"].rearrange("(fc p) n -> p fc n", p=128), pfx + "wlg")
    p.wait("vector", pfx + "wl", p.val(pfx + "wl"))
    p.op("vector", "tensor_scalar", bk2[:, :], bk[:, :], C2, None, ALU.mult)

    W = {n: p.sbuf(pfx + "W" + n, [128, CH, C], BF16) for n in ["q", "k", "v"]}
    xt = p.sbuf(pfx + "xt", [128, CH, TS], F32)
    ps = make_ln_scratch_sb(p, TS, pfx)
    yt = p.sbuf(pfx + "yt", [128, CH, TS], BF16)
    qst = p.sbuf(pfx + "qst", [128, 4, TS], BF16)
    vst = p.sbuf(pfx + "vst", [128, C], BF16)
    QTh = p.sbuf(pfx + "QTh", [128, SL], BF16)
    KTh = p.sbuf(pfx + "KTh", [128, SL], BF16)
    Vh = p.sbuf(pfx + "Vh", [128, 16, 128], BF16)
    qq = p.sbuf(pfx + "qq", [8, SL], F32)
    kq = p.sbuf(pfx + "kq", [8, SL], F32)
    qqn = p.sbuf(pfx + "qqn", [8, SL], BF16)
    qqh = p.sbuf(pfx + "qqh", [8, SL], BF16)
    kqn = p.sbuf(pfx + "kqn", [8, SL], BF16)
    sq8 = p.sbuf(pfx + "sq8", [8, TS], F32)
    sqb = p.sbuf(pfx + "sqb", [128, TS], BF16)
    rows = p.sbuf(pfx + "rows", [128, SL], F32)
    mrow = p.sbuf(pfx + "mrow", [1, 8], F32)
    nmc = p.sbuf(pfx + "nmc", [128, 1], F32)
    rsb = p.sbuf(pfx + "rsb", [128, TS], F32)
    es = p.sbuf(pfx + "es", [128, 16, TS], BF16)
    scsd = p.sbuf(pfx + "scs", [128, 2, TS], F32)
    sctd = p.sbuf(pfx + "sct", [128, 2, TS], F32)
    x1f = p.sbuf(pfx + "x1f", [128, CH, TS], F32)
    pA = p.psum(pfx + "pA")
    pB = p.psum(pfx + "pB")
    pC = p.psum(pfx + "pC")
    pD = p.psum(pfx + "pD")
    pE = p.psum(pfx + "pE")
    pF = p.psum(pfx + "pF")
    ps["psS"], ps["psQ"], ps["psRB"], ps["psBB"] = pA, pB, pC, pD
    ps["gk"] = {"S": "gA", "Q": "gB", "RB": "gC", "BB": "gD",
                "xb": pfx + "lnxb_g", "tscr": pfx + "lntscr_g"}
    rQ = rows[0:1, 0:SL]
    rK = rows[0:1, 0:SL]
    rX = rows[32:33, 0:SL]
    x1p = ps["tscr"]
    pj = [pE, pF]
    pjk = ["gE", "gF"]
    pab = [pC, pD]
    pabk = ["gC", "gD"]

    def mmg(i, cols, pairs, banks=pj, keys=pjk, rowlim=None):
        k = keys[i % 2]
        p.wait_buf("tensor", k)
        bank = banks[i % 2]
        ap = bank[0:rowlim, cols] if rowlim else bank[:, cols]
        return p.mm(ap, pairs, sem_name=smm), bank, k

    for b in range(4):
        t0 = b * SL
        for n, key in [("q", "a1wq"), ("k", "a1wk"), ("v", "a1wv")]:
            p.wait_buf("gpsimd", pfx + "W" + n)
            p.dma("gpsimd", W[n][:, :, :],
                  w_in[key].rearrange("(ch p) d -> p ch d", p=128), pfx + "wlg")
        p.wait("tensor", pfx + "wlg", p.val(pfx + "wlg"))
        for tb in range(4):
            p.wait_buf("sync", pfx + "xt")
            tgt = p.dma("sync", xt[:, :, :],
                        src_r[:, :, t0 + tb * TS:t0 + (tb + 1) * TS], pfx + "xin")
            p.wait("vector", pfx + "xin", tgt)
            p.wait_buf("vector", pfx + "yf")
            ytail = emit_ln_tile(p, ps, xt[:, :, :], x1f[:, :, :], TS)
            p.set_guard(pfx + "xt", "lndve", ytail)
            p.wait_buf("vector", pfx + "yt")
            tyc = p.op("vector", "tensor_copy", yt[:, :, :], x1f[:, :, :],
                       sem_name=sdve)
            p.wait("sync", sdve, tyc)
            tyd = p.dma("sync", yr_d[:, :, t0 + tb * TS:t0 + (tb + 1) * TS],
                        x1f[:, :, :], pfx + "qsd")
            p.set_guard(pfx + "yf", pfx + "qsd", tyd)
            p.wait("tensor", sdve, tyc)
            for wname, dmat, sc1_, sc2_ in [("q", qt_d, None, None),
                                            ("k", kt_d, C2, None)]:
                for half in range(2):
                    p.wait_buf("tensor", pfx + "qst_dma")
                    last = None
                    for dci in range(4):
                        dc = half * 4 + dci
                        tmm, bank, bk_ = mmg(dc, slice(0, TS),
                                             [(wt_ap(W[wname], ch, dc),
                                               yt[:, ch, :])
                                              for ch in range(CH)])
                        if wname == "q" and dc == CH - 1:
                            p.set_guard(pfx + "Wq_inuse", smm, tmm)
                        p.wait("vector", smm, tmm)
                        if wname == "q":
                            last = p.op("vector", "tensor_scalar",
                                        qst[:, dci, :], bank[:, 0:TS],
                                        bq[:, dc:dc + 1], None, ALU.add,
                                        sem_name=sdve)
                        else:
                            last = p.op("vector", "tensor_scalar",
                                        qst[:, dci, :], bank[:, 0:TS], C2,
                                        bk2[:, dc:dc + 1], ALU.mult, ALU.add,
                                        sem_name=sdve)
                        p.set_guard(bk_, sdve, last)
                    p.wait("sync", sdve, last)
                    tq = p.dma("sync",
                               dmat[:, tb * TS:(tb + 1) * TS]
                               .rearrange("(ch p) t -> p ch t", p=128)
                               [:, half * 4:(half + 1) * 4, :],
                               qst[:, :, :], pfx + "qsd")
                    p.set_guard(pfx + "qst_dma", pfx + "qsd", tq)
            for tc in range(4):
                p.wait_buf("sync", pfx + "vst_dma")
                for dh in range(2):
                    pairs = [(yt[:, ch, tc * 128:(tc + 1) * 128],
                              W["v"][:, ch, dh * TS:(dh + 1) * TS])
                             for ch in range(CH)]
                    pairs.append((p.ones1b[:, :],
                                  bvr[0:1, dh * TS:(dh + 1) * TS]))
                    tmm, bank, bk_ = mmg(dh, slice(0, TS), pairs)
                    p.wait("vector", smm, tmm)
                    last = p.op("vector", "tensor_copy",
                                vst[:, dh * TS:(dh + 1) * TS], bank[:, 0:TS],
                                sem_name=sdve)
                    p.set_guard(bk_, sdve, last)
                p.wait("sync", sdve, last)
                tv = p.dma("sync",
                           v_d[tb * TS + tc * 128:tb * TS + tc * 128 + 128, :],
                           vst[:, :], pfx + "qsd")
                p.set_guard(pfx + "vst_dma", pfx + "qsd", tv)
            p.set_guard(pfx + "yt", smm, p.val(smm))
        p.wait_buf("gpsimd", pfx + "Wq_inuse")
        p.wait_buf("gpsimd", pfx + "Wq")
        p.dma("gpsimd", W["q"][:, :, :],
              w_in["a1wo"].rearrange("(ch p) d -> p ch d", p=128), pfx + "wlg")
        two = p.val(pfx + "wlg")
        for h in range(8):
            p.wait_buf("sync", pfx + "QTh")
            tqh = p.dma("sync", QTh[:, :], qt_d[h * 128:(h + 1) * 128, :],
                        pfx + "hld")
            p.wait_buf("sync", pfx + "KTh")
            tkh = p.dma("sync", KTh[:, :], kt_d[h * 128:(h + 1) * 128, :],
                        pfx + "hld")
            p.wait_buf("gpsimd", pfx + "Vh")
            if h == 0:
                p.wait("gpsimd", pfx + "qsd", p.val(pfx + "qsd"))
            tvh = p.dma("gpsimd", Vh[:, :, :],
                        v_d.rearrange("(kt p) d -> p kt d", p=128)
                        [:, :, h * 128:(h + 1) * 128], pfx + "vld")
            p.wait("tensor", pfx + "hld", tqh)
            p.wait_buf("vector", pfx + "qq")
            for qb in range(4):
                tmm, bank, bk_ = mmg(qb, slice(0, TS),
                                     [(wc[:, h, :], QTh[:, qb * TS:(qb + 1) * TS])],
                                     banks=pab, keys=pabk, rowlim=8)
                p.wait("vector", smm, tmm)
                td = p.op("vector", "tensor_scalar",
                          qq[:, qb * TS:(qb + 1) * TS], bank[0:8, 0:TS],
                          bc2[:, h:h + 1], None, ALU.add, sem_name=sdve)
                p.set_guard(bk_, sdve, td)
            p.wait("tensor", pfx + "hld", tkh)
            p.wait_buf("vector", pfx + "kq")
            for qb in range(4):
                tmm, bank, bk_ = mmg(qb, slice(0, TS),
                                     [(wc[:, h, :], KTh[:, qb * TS:(qb + 1) * TS])],
                                     banks=pab, keys=pabk, rowlim=8)
                p.wait("vector", smm, tmm)
                td = p.op("vector", "tensor_scalar",
                          kq[:, qb * TS:(qb + 1) * TS], bank[0:8, 0:TS],
                          bc2[:, h:h + 1], None, ALU.add, sem_name=sdve)
                p.set_guard(bk_, sdve, td)
            for src_t, dsts in [(qq, [(qqn, None), (qqh, 0.5)]),
                                (kq, [(kqn, None)])]:
                p.wait_buf("vector", pfx + "rrow")
                for qb in range(4):
                    p.wait("scalar", sdve, p.val(sdve))
                    p.wait_buf("scalar", pfx + "sq8")
                    ta = p.op("scalar", "activation", sq8[:, :],
                              src_t[:, qb * TS:(qb + 1) * TS], AF.Square,
                              sem_name=sact)
                    p.wait("tensor", sact, ta)
                    p.wait_buf("tensor", "gA")
                    tmm = p.mm(pA[0:1, 0:TS],
                               [(p.onescf[0:8, :], sq8[:, :])], sem_name=smm)
                    p.set_guard(pfx + "sq8", smm, tmm)
                    p.wait("vector", smm, tmm)
                    td = p.op("vector", "tensor_copy",
                              rQ[0:1, qb * TS:(qb + 1) * TS], pA[0:1, 0:TS],
                              sem_name=sdve)
                    p.set_guard("gA", sdve, td)
                p.wait("scalar", sdve, p.val(sdve))
                p.op("scalar", "activation", rQ, rQ, AF.Ln)
                trk = p.op("scalar", "activation", rQ, rQ, AF.Exp, scale=-0.5,
                           sem_name=sact)
                p.wait("tensor", sact, trk)
                for qb in range(4):
                    sl = slice(qb * TS, (qb + 1) * TS)
                    p.wait_buf("tensor", "gB")
                    tmm = p.mm(pB[0:8, 0:TS],
                               [(p.ones1f[0:1, 0:8], rQ[0:1, sl])],
                               sem_name=smm)
                    p.set_guard(pfx + "rrow", smm, tmm)
                    p.wait("vector", smm, tmm)
                    td = None
                    for dst_t, scale in dsts:
                        if scale is None:
                            td = p.op("vector", "tensor_tensor", dst_t[:, sl],
                                      src_t[:, sl], pB[0:8, 0:TS], ALU.mult,
                                      sem_name=sdve)
                        else:
                            td = p.op("vector", "scalar_tensor_tensor",
                                      dst_t[:, sl], src_t[:, sl], scale,
                                      pB[0:8, 0:TS], ALU.mult, ALU.mult,
                                      sem_name=sdve)
                    p.set_guard("gB", sdve, td)
            for i, srcT in enumerate([QTh, KTh]):
                for qb in range(4):
                    sl = slice(qb * TS, (qb + 1) * TS)
                    p.wait_buf("scalar", pfx + "sqb")
                    ta = p.op("scalar", "activation", sqb[:, :], srcT[:, sl],
                              AF.Square, sem_name=sact)
                    p.wait("tensor", sact, ta)
                    p.wait_buf("tensor", "gA")
                    tmm = p.mm(pA[0:1, 0:TS],
                               [(p.onescb[:, :], sqb[:, :])], sem_name=smm)
                    p.set_guard(pfx + "sqb", smm, tmm)
                    p.wait("vector", smm, tmm)
                    td = p.op("vector", "tensor_copy", rX[0:1, sl],
                              pA[0:1, 0:TS], sem_name=sdve)
                    p.set_guard("gA", sdve, td)
                tv2 = p.op("vector", "reduce_max", mrow[0:1, i:i + 1], rX,
                           mybir.AxisListType.X, sem_name=sdve)
            p.wait("scalar", sdve, tv2)
            ta = p.op("scalar", "activation", mrow[0:1, 0:2], mrow[0:1, 0:2],
                      AF.Sqrt, sem_name=sact)
            p.wait("vector", sact, ta)
            p.op("vector", "scalar_tensor_tensor", mrow[0:1, 2:3],
                 mrow[0:1, 0:1], -1.0, mrow[0:1, 1:2], ALU.mult, ALU.mult)
            tv2 = p.op("vector", "tensor_scalar", mrow[0:1, 2:3],
                       mrow[0:1, 2:3], -0.55, None, ALU.add, sem_name=sdve)
            p.wait("tensor", sdve, tv2)
            p.wait_buf("tensor", "gB")
            tmm = p.mm(pB[:, 0:1], [(p.ones1f[:, :], mrow[0:1, 2:3])],
                       sem_name=smm)
            p.wait("vector", smm, tmm)
            p.wait_buf("vector", pfx + "nmc")
            tnm = p.op("vector", "tensor_copy", nmc[:, :], pB[:, 0:1],
                       sem_name=sdve)
            p.set_guard("gB", sdve, tnm)
            for qb in range(4):
                sl = slice(qb * TS, (qb + 1) * TS)
                pend = []
                grp = []
                for kt in range(16):
                    ksl = slice(kt * 128, (kt + 1) * 128)
                    par = kt % 2
                    tA, bankA, kA = mmg(kt, slice(0, TS),
                                        [(kqn[:, ksl], qqn[:, sl])],
                                        banks=pab, keys=pabk, rowlim=128)
                    tB, bankB, kB = mmg(kt, slice(0, TS),
                                        [(KTh[:, ksl], QTh[:, sl]),
                                         (kqn[:, ksl], qqh[:, sl])])
                    if kt == 15 and qb == 3:
                        p.set_guard(pfx + "KTh", smm, tB)
                    grp.append((kt, par, tA, tB, bankA, bankB, kA, kB))
                    if par == 0:
                        continue
                    tsin = {}
                    for (k_, p_, tA_, tB_, bA_, bB_, kA_, kB_) in grp:
                        p.wait("scalar", smm, tA_)
                        p.wait_buf("scalar", pfx + f"scsV{p_}")
                        tsin[k_] = p.op("scalar", "activation",
                                        scsd[:, p_, :], bA_[:, 0:TS], AF.Sin,
                                        scale=float(math.pi), sem_name=sact)
                    for (kp, pp, sctp, tdp) in pend:
                        p.wait("scalar", sdve, tdp)
                        ta = p.op("scalar", "activation", es[:, kp, :], sctp,
                                  AF.Exp, bias=nmc[:, 0:1], sem_name=sact)
                        p.set_guard(pfx + f"sctP{pp}", sact, ta)
                    pend = []
                    for (k_, p_, tA_, tB_, bA_, bB_, kA_, kB_) in grp:
                        sct = sctd[:, p_, :]
                        p.wait("vector", sact, tsin[k_])
                        p.wait("vector", smm, tB_)
                        p.wait_buf("vector", pfx + f"sctP{p_}")
                        p.op("vector", "scalar_tensor_tensor", sct,
                             scsd[:, p_, :], 0.05, bA_[:, 0:TS],
                             ALU.mult, ALU.mult)
                        td = p.op("vector", "tensor_tensor", sct, sct,
                                  bB_[:, 0:TS], ALU.add, sem_name=sdve)
                        p.set_guard(kA_, sdve, td)
                        p.set_guard(kB_, sdve, td)
                        p.set_guard(pfx + f"scsV{p_}", sdve, td)
                        pend.append((k_, p_, sct, td))
                    grp = []
                for (kp, pp, sctp, tdp) in pend:
                    p.wait("scalar", sdve, tdp)
                    ta = p.op("scalar", "activation", es[:, kp, :], sctp,
                              AF.Exp, bias=nmc[:, 0:1], sem_name=sact)
                    p.set_guard(pfx + f"sctP{pp}", sact, ta)
                pend = []
                p.wait("tensor", sact, p.val(sact))
                p.wait_buf("tensor", "gA")
                tmm = p.mm(pA[0:1, 0:TS],
                           [(p.onescb[:, :], es[:, kt, :]) for kt in range(16)],
                           sem_name=smm)
                p.wait("scalar", smm, tmm)
                rr_ = scsd[0:1, 0, 0:TS]
                p.wait_buf("scalar", pfx + "scsV0")
                p.op("scalar", "activation", rr_, pA[0:1, 0:TS], AF.Ln)
                ta = p.op("scalar", "activation", rr_, rr_, AF.Exp, scale=-1.0,
                          sem_name=sact)
                p.set_guard("gA", sact, ta)
                p.wait("tensor", sact, ta)
                p.wait_buf("tensor", "gB")
                tmm = p.mm(pB[:, 0:TS], [(p.ones1f[:, :], rr_)], sem_name=smm)
                p.set_guard(pfx + "scsV0", smm, tmm)
                p.wait("vector", smm, tmm)
                p.wait_buf("vector", pfx + "rsb")
                trs = p.op("vector", "tensor_copy", rsb[:, :], pB[:, 0:TS],
                           sem_name=sdve)
                p.set_guard("gB", sdve, trs)
                if qb == 0:
                    p.wait("tensor", pfx + "vld", tvh)
                tmm, bank, bk_ = mmg(qb, slice(0, TS),
                                     [(Vh[:, kt, :], es[:, kt, :])
                                      for kt in range(16)])
                if qb == 3:
                    p.set_guard(pfx + "Vh", smm, tmm)
                    p.set_guard(pfx + "es_last", smm, tmm)
                p.wait("vector", smm, tmm)
                p.wait_buf("vector", pfx + "qst_dma")
                td = p.op("vector", "tensor_tensor", qst[:, 0, :],
                          bank[:, 0:TS], rsb[:, :], ALU.mult, sem_name=sdve)
                p.set_guard(bk_, sdve, td)
                p.wait("sync", sdve, td)
                tst = p.dma("sync", at_d[h * 128:(h + 1) * 128,
                                         t0 + qb * TS:t0 + (qb + 1) * TS],
                            qst[:, 0, :], pfx + "qsd")
                p.set_guard(pfx + "qst_dma", pfx + "qsd", tst)
        p.wait("tensor", pfx + "wlg", two)
        for tb in range(4):
            p.wait_buf("sync", pfx + "xt")
            tgt = p.dma("sync", xt[:, :, :],
                        yr_d[:, :, t0 + tb * TS:t0 + (tb + 1) * TS], pfx + "xin")
            p.wait_buf("gpsimd", pfx + "at_ld")
            p.wait_buf("gpsimd", pfx + "qst_dma")
            tat = p.dma("gpsimd", yt[:, :, :],
                        ar[:, :, t0 + tb * TS:t0 + (tb + 1) * TS], pfx + "atl")
            p.wait("tensor", pfx + "atl", tat)
            p.wait("vector", pfx + "xin", tgt)
            p.wait_buf("vector", pfx + "lntscr_g")
            p.wait_buf("vector", pfx + "yf")
            td = None
            for ec in range(CH):
                tmm, bank, bk_ = mmg(ec, slice(0, TS),
                                     [(wt_ap(W["q"], ch, ec), yt[:, ch, :])
                                      for ch in range(CH)])
                p.wait("vector", smm, tmm)
                td = p.op("vector", "tensor_scalar", x1p[:, ec, :],
                          bank[:, 0:TS], bo[:, ec:ec + 1], None, ALU.add)
                td = p.op("vector", "tensor_tensor", x1p[:, ec, :],
                          x1p[:, ec, :], xt[:, ec, :], ALU.add, sem_name=sdve)
                p.set_guard(bk_, sdve, td)
            p.set_guard(pfx + "at_ld", smm, p.val(smm))
            p.set_guard(pfx + "xt", sdve, td)
            p.set_guard(pfx + "yt", smm, p.val(smm))
            ytail = emit_ln_tile(p, ps, x1p[:, :, :], x1f[:, :, :], TS,
                                 gain_col=[g1[:, i:i + 1] for i in range(CH)],
                                 beta_col=[be1[:, i:i + 1] for i in range(CH)])
            p.set_guard(pfx + "lntscr_g", "lndve", ytail)
            p.wait("sync", "lndve", ytail)
            p.wait("gpsimd", "lndve", ytail)
            p.wait_buf("sync", pfx + "x1f_dma")
            t1f = p.dma("sync", x1fr[:, :, t0 + tb * TS:t0 + (tb + 1) * TS],
                        x1f[:, :, :], pfx + "qsd")
            t1b = p.dma("gpsimd", x1br[:, :, t0 + tb * TS:t0 + (tb + 1) * TS],
                        x1f[:, :, :], pfx + "x1b")
            p.set_guard(pfx + "x1f_dma", pfx + "x1b", t1b)
            p.wait("vector", pfx + "qsd", t1f)
        p.set_guard(pfx + "Wq", smm, p.val(smm))
        p.set_guard(pfx + "Wk", smm, p.val(smm))
        p.set_guard(pfx + "Wv", smm, p.val(smm))
    for n in ["x1f", "sct", "scs", "es", "rsb", "nmc", "mrow", "rows", "sqb",
              "sq8", "kqn", "qqh", "qqn", "kq", "qq", "Vh", "KTh", "QTh",
              "vst", "qst", "yt"]:
        p.sfree(pfx + n)
    for n in ["lnrows2", "lnrows1", "lntscr", "lnxb"]:
        p.sfree(pfx + n)
    for n in ["xt", "Wv", "Wk", "Wq"]:
        p.sfree(pfx + n)

    wqo = p.sbuf(pfx + "wqo", [8, FF], BF16)
    p.dma("gpsimd", wqo[:, :], w_in["a1wqo"][:, :], pfx + "wlg")
    w1c = [p.sbuf(pfx + f"w1c{i}", [128, CH, 128], BF16) for i in range(2)]
    w2c = [p.sbuf(pfx + f"w2c{i}", [128, 32, 128], BF16) for i in range(2)]
    h1T = p.sbuf(pfx + "h1T", [128, 32, TS], BF16)
    x1bt = p.sbuf(pfx + "x1bt", [128, CH, TS], BF16)
    x1ft = p.sbuf(pfx + "x1ft", [128, CH, TS], F32)
    qm = p.sbuf(pfx + "qm", [8, TS], F32)
    qt1 = p.sbuf(pfx + "qt1", [8, TS], F32)
    qu = p.sbuf(pfx + "qu", [8, TS], F32)
    qrr = p.sbuf(pfx + "qrr", [8, TS], F32)
    qeb = p.sbuf(pfx + "qeb", [8, TS], BF16)
    x2p = p.sbuf(pfx + "x2p", [128, CH, TS], F32)
    psn = make_ln_scratch_sb(p, TS, pfx + "n")
    psn["psS"], psn["psQ"], psn["psRB"], psn["psBB"] = pA, pB, pC, pD
    psn["gk"] = {"S": "gA", "Q": "gB", "RB": "gC", "BB": "gD",
                 "xb": pfx + "nlnxb_g", "tscr": pfx + "nlntscr_g"}
    o2 = p.sbuf(pfx + "o2", [128, CH, TS], F32)
    x4t = p.sbuf(pfx + "x4t", [128, CH, TS], F32)
    w1r = w_in["a1w1"].rearrange("(ch p) f -> p ch f", p=128)
    w2r = w_in["a1w2"].rearrange("(fc p) e -> p fc e", p=128)

    for k in range(16):
        t0 = k * TS
        p.wait_buf("sync", pfx + "x1bt")
        tb1 = p.dma("sync", x1bt[:, :, :], x1br[:, :, t0:t0 + TS], pfx + "fin")
        p.wait_buf("sync", pfx + "x1ft")
        tf1 = p.dma("sync", x1ft[:, :, :], x1fr[:, :, t0:t0 + TS], pfx + "fin")
        p.wait_buf("sync", pfx + "x4t")
        tx4 = p.dma("sync", x4t[:, :, :], src_r[:, :, t0:t0 + TS], pfx + "fin")
        p.wait("tensor", pfx + "fin", tb1)
        p.wait("tensor", pfx + "wlg", p.val(pfx + "wlg"))
        lastev = None
        for fc in range(32):
            wb = w1c[fc % 2]
            p.wait_buf("gpsimd", pfx + f"w1c{fc % 2}")
            tw = p.dma("gpsimd", wb[:, :, :],
                       w1r[:, :, fc * 128:(fc + 1) * 128], pfx + "fw")
            p.wait("tensor", pfx + "fw", tw)
            tmm, bank, bk_ = mmg(fc, slice(0, TS),
                                 [(wb[:, ch, :], x1bt[:, ch, :])
                                  for ch in range(CH)])
            p.set_guard(pfx + f"w1c{fc % 2}", smm, tmm)
            p.wait("vector", smm, tmm)
            lastev = p.op("vector", "tensor_scalar", h1T[:, fc, :],
                          bank[:, 0:TS], b1c[:, fc:fc + 1], None, ALU.add,
                          sem_name=sdve)
            p.set_guard(bk_, sdve, lastev)
        p.wait("tensor", sdve, lastev)
        p.wait_buf("tensor", "gC")
        tmm = p.mm(pC[0:8, 0:TS],
                   [(wm[:, fc, :], h1T[:, fc, :]) for fc in range(32)],
                   sem_name=smm)
        p.wait("vector", smm, tmm)
        p.wait_buf("vector", pfx + "qm")
        tqm = p.op("vector", "tensor_scalar", qm[:, :], pC[0:8, 0:TS],
                   bmc[:, 0:1], None, ALU.add, sem_name=sdve)
        p.set_guard("gC", sdve, tqm)
        p.wait("scalar", sdve, tqm)
        p.wait_buf("scalar", pfx + "qt1")
        ta1 = p.op("scalar", "activation", qt1[:, :], qm[:, :], AF.Tanh,
                   sem_name=sact)
        p.op("vector", "tensor_scalar", qu[:, :], qm[:, :], 0.125, 0.25,
             ALU.mult, ALU.add)
        p.set_guard(pfx + "qm", sdve if False else sdve, p.val(sdve))
        p.op("vector", "tensor_scalar", qrr[:, :], qu[:, :], MAGIC, MAGIC,
             ALU.add, ALU.subtract)
        tdu = p.op("vector", "tensor_tensor", qu[:, :], qu[:, :], qrr[:, :],
                   ALU.subtract, sem_name=sdve)
        p.wait("scalar", sdve, tdu)
        p.wait_buf("scalar", pfx + "qrr")
        ta2 = p.op("scalar", "activation", qrr[:, :], qu[:, :], AF.Sin,
                   scale=float(2 * math.pi), sem_name=sact)
        p.set_guard(pfx + "qu", sact, ta2)
        p.wait("vector", sact, ta2)
        tqe = p.op("vector", "tensor_tensor", qeb[:, :], qt1[:, :], qrr[:, :],
                   ALU.mult, sem_name=sdve)
        p.set_guard(pfx + "qt1", sdve, tqe)
        p.set_guard(pfx + "qrr", sdve, tqe)
        p.wait("tensor", sdve, tqe)
        lastev = None
        for fc in range(32):
            tmm, bank, bk_ = mmg(fc, slice(0, TS),
                                 [(wqo[:, fc * 128:(fc + 1) * 128], qeb[:, :])])
            if fc == 31:
                p.set_guard(pfx + "qeb", smm, tmm)
            p.wait("vector", smm, tmm)
            lastev = p.op("vector", "tensor_scalar", h1T[:, fc, :],
                          bank[:, 0:TS], bqoc[:, fc:fc + 1], None, ALU.add,
                          sem_name=sdve)
            p.set_guard(bk_, sdve, lastev)
        p.wait("tensor", sdve, lastev)
        p.wait("vector", pfx + "fin", tf1)
        td = None
        for ec in range(CH):
            wb = w2c[ec % 2]
            p.wait_buf("gpsimd", pfx + f"w2c{ec % 2}")
            tw = p.dma("gpsimd", wb[:, :, :],
                       w2r[:, :, ec * 128:(ec + 1) * 128], pfx + "fw")
            p.wait("tensor", pfx + "fw", tw)
            tmm, bank, bk_ = mmg(ec, slice(0, TS),
                                 [(wb[:, fc, :], h1T[:, fc, :])
                                  for fc in range(32)])
            p.set_guard(pfx + f"w2c{ec % 2}", smm, tmm)
            if ec == CH - 1:
                p.set_guard(pfx + "h1T", smm, tmm)
            p.wait("vector", smm, tmm)
            p.op("vector", "tensor_scalar", x2p[:, ec, :], bank[:, 0:TS],
                 b2c[:, ec:ec + 1], None, ALU.add)
            td = p.op("vector", "tensor_tensor", x2p[:, ec, :], x2p[:, ec, :],
                      x1ft[:, ec, :], ALU.add, sem_name=sdve)
            p.set_guard(bk_, sdve, td)
        p.set_guard(pfx + "x1ft", sdve, td)
        p.wait_buf("vector", pfx + "o2")
        ytail = emit_ln_tile(p, psn, x2p[:, :, :], o2[:, :, :], TS,
                             gain_col=[g2[:, i:i + 1] for i in range(CH)],
                             beta_col=[be2[:, i:i + 1] for i in range(CH)])
        p.set_guard(pfx + "x2p_g", "lndve", ytail)
        p.wait("vector", pfx + "fin", tx4)
        tdo = None
        for ch2 in range(CH):
            tdo = p.op("vector", "tensor_tensor", o2[:, ch2, :], o2[:, ch2, :],
                       x4t[:, ch2, :], ALU.add,
                       sem_name=sdve if ch2 == CH - 1 else None)
        p.set_guard(pfx + "x4t", sdve, tdo)
        p.wait("sync", sdve, tdo)
        tso = p.dma("sync", dst_r[:, :, t0:t0 + TS], o2[:, :, :], pfx + "xo")
        p.set_guard(pfx + "o2", pfx + "xo", tso)
        p.set_guard(pfx + "x1bt", smm, p.val(smm))
        p.set_guard(pfx + "qm", sdve, p.val(sdve))
    p.wait("sync", pfx + "xo", p.val(pfx + "xo"))


def make_ln_scratch_sb(p, W, pfx=""):
    return dict(
        xb=p.sbuf(pfx + "lnxb", [128, CH, W], BF16),
        tscr=p.sbuf(pfx + "lntscr", [128, CH, W], F32),
        rows1=p.sbuf(pfx + "lnrows1", [128, W], F32),
        rows2=p.sbuf(pfx + "lnrows2", [128, 2 * W], F32),
    )


def build_a1():
    p = Prog("pa1")
    nc = p.nc
    x_in = nc.dram_tensor("x", [C, TOK], F32, kind="ExternalInput").ap()
    x_out = nc.dram_tensor("xo", [C, TOK], F32, kind="ExternalOutput").ap()
    w_in = {}
    for n, shape in [("a1wq", [C, C]), ("a1wk", [C, C]), ("a1wv", [C, C]),
                     ("a1wo", [C, C]), ("a1wc", [C, NQ]), ("a1w1", [C, FF]),
                     ("a1wm", [FF, NQ]), ("a1wqo", [NQ, FF]), ("a1w2", [FF, C])]:
        w_in[n] = nc.dram_tensor(n, shape, F32, kind="ExternalInput").ap()
    for n, shape in [("a1bq", [128, CH]), ("a1bk", [128, CH]),
                     ("a1bo", [128, CH]), ("a1g1", [128, CH]),
                     ("a1be1", [128, CH]), ("a1g2", [128, CH]),
                     ("a1be2", [128, CH]), ("a1b1", [128, 32]),
                     ("a1bqo", [128, 32]), ("a1b2", [128, CH]),
                     ("a1bc2", [8, 8]), ("a1bm", [8, 1]), ("a1bv", [1, C])]:
        w_in[n] = nc.dram_tensor(n, shape, F32, kind="ExternalInput").ap()
    dram = dict(
        y=nc.dram_tensor("yd", [C, TOK], F32, kind="Internal").ap(),
        qt=nc.dram_tensor("qtd", [C, SL], BF16, kind="Internal").ap(),
        kt=nc.dram_tensor("ktd", [C, SL], BF16, kind="Internal").ap(),
        v=nc.dram_tensor("vd", [SL, C], BF16, kind="Internal").ap(),
        at=nc.dram_tensor("atd", [C, TOK], BF16, kind="Internal").ap(),
        x1f=nc.dram_tensor("x1fd", [C, TOK], F32, kind="Internal").ap(),
        x1b=nc.dram_tensor("x1bd", [C, TOK], BF16, kind="Internal").ap(),
    )
    xr_in = x_in.rearrange("(ch p) t -> p ch t", p=128)
    r_out = x_out.rearrange("(ch p) t -> p ch t", p=128)
    p.init_stage()
    p.stage("a1")
    emit_a1(p, xr_in, r_out, w_in, dram)
    return p.finalize()


def a1_weight_inputs(full_inp):
    g = lambda k: np.asarray(full_inp["L1_attn_" + k], dtype=np.float32)
    d = {}
    d["a1wq"] = np.ascontiguousarray(g("Wq").transpose(1, 0, 2).reshape(C, C))
    d["a1wk"] = np.ascontiguousarray(g("Wk").transpose(1, 0, 2).reshape(C, C))
    d["a1wv"] = np.ascontiguousarray(g("Wv").transpose(1, 0, 2).reshape(C, C))
    d["a1wo"] = np.ascontiguousarray(g("Wo"))
    d["a1wc"] = np.ascontiguousarray(g("Wc2q").transpose(1, 0, 2).reshape(C, NQ))
    d["a1w1"] = np.ascontiguousarray(g("W1"))
    d["a1wm"] = np.ascontiguousarray(g("Wm"))
    d["a1wqo"] = np.ascontiguousarray(g("Wqo"))
    d["a1w2"] = np.ascontiguousarray(g("W2"))
    d["a1bq"] = np.ascontiguousarray(g("bq").T)
    d["a1bk"] = np.ascontiguousarray(g("bk").T)
    d["a1bv"] = np.ascontiguousarray(g("bv").reshape(1, C))
    d["a1bo"] = _packcol(g("bo"))
    d["a1g1"] = _packcol(g("g1"))
    d["a1be1"] = _packcol(g("beta1"))
    d["a1g2"] = _packcol(g("g2"))
    d["a1be2"] = _packcol(g("beta2"))
    d["a1b1"] = _packcol(g("b1"))
    d["a1bqo"] = _packcol(g("bqo"))
    d["a1b2"] = _packcol(g("b2"))
    d["a1bc2"] = np.ascontiguousarray(g("bc2q").T)
    d["a1bm"] = np.ascontiguousarray(g("bm").reshape(NQ, 1))
    return d


def run_a1(x_slices, wd):
    nc = _prog("a1", build_a1)
    in_maps = [dict(x=x_slices[i], **wd) for i in range(NCORES)]
    res = run_bass_kernel_spmd(nc, in_maps, core_ids=list(range(NCORES)))
    return [r["xo"] for r in res.results]



def _featmaj(x3):
    return np.ascontiguousarray(x3.transpose(2, 0, 1).reshape(C, -1))


def _packcol(v):
    n = v.shape[0] // 128
    return np.ascontiguousarray(np.asarray(v, dtype=np.float32).reshape(n, 128).T)


def f1_weight_inputs(full_inp):
    g = lambda k: np.asarray(full_inp["L1_ffn_" + k], dtype=np.float32)
    d = {}
    for n in ["wf", "uf", "wi", "ui", "wc", "uc", "wo", "uo"]:
        d[n] = np.ascontiguousarray(g(n).T)
    for n in ["bf", "bi", "bc", "bo"]:
        d[n] = _packcol(g(n))
    c0 = np.broadcast_to(_packcol(g("in_c"))[:, :, None], (128, CH, PP))
    d["c0"] = np.ascontiguousarray(c0.reshape(128, CH * PP))
    return d


def l0_weight_inputs(full_inp):
    g = lambda k: np.asarray(full_inp[k], dtype=np.float32)
    d = {}
    d["f0r"] = np.ascontiguousarray(g("L0_ffn_W_r").T)
    d["f0k"] = np.ascontiguousarray(g("L0_ffn_W_k").T)
    d["f0v"] = np.ascontiguousarray(g("L0_ffn_W_v").T)
    d["a0q"] = np.ascontiguousarray(g("L0_attn_Wq").T)
    d["a0k"] = np.ascontiguousarray(g("L0_attn_Wk").T)
    d["a0v"] = np.ascontiguousarray(g("L0_attn_Wv").T)
    d["a0o"] = np.ascontiguousarray(g("L0_attn_Wo").T)
    d["a0bq"] = _packcol(g("L0_attn_bq"))
    d["a0bk"] = _packcol(g("L0_attn_bk"))
    d["a0bo"] = _packcol(g("L0_attn_bo"))
    d["a0bv"] = np.ascontiguousarray(g("L0_attn_bv").reshape(1, C))
    d["mur"] = _packcol(g("L0_ffn_mu_r"))
    d["muk"] = _packcol(g("L0_ffn_mu_k"))
    return d


_PROGS = {}


def _prog(name, fn):
    if name not in _PROGS:
        _PROGS[name] = fn()
    return _PROGS[name]


def run_f1(x_slices, wd):
    nc = _prog("f1", build_f1)
    in_maps = [dict(x=x_slices[i], **wd) for i in range(NCORES)]
    res = run_bass_kernel_spmd(nc, in_maps, core_ids=list(range(NCORES)))
    return [r["xo"] for r in res.results]


def run_l0(x_slices, wd):
    nc = _prog("l0", build_l0)
    in_maps = [dict(x=x_slices[i], **wd) for i in range(NCORES)]
    res = run_bass_kernel_spmd(nc, in_maps, core_ids=list(range(NCORES)))
    return [r["xo"] for r in res.results]


def kernel(**inputs):
    x0 = np.ascontiguousarray(np.asarray(inputs["L0_attn_x"], dtype=np.float32))
    l0w = l0_weight_inputs(inputs)
    f1w = f1_weight_inputs(inputs)
    a1w = a1_weight_inputs(inputs)

    def bslices(x_all):
        return [np.ascontiguousarray(
            x_all[:, 4 * i:4 * (i + 1), :].reshape(C, TOK)) for i in range(NCORES)]

    def pslices(x_all):
        return [np.ascontiguousarray(
            x_all[:, :, 256 * j:256 * (j + 1)].reshape(C, TOK))
            for j in range(NCORES)]

    def from_b(outs):
        return np.concatenate([o.reshape(C, 4, SL) for o in outs], axis=1)

    def from_p(outs):
        return np.concatenate([o.reshape(C, NB, PP) for o in outs], axis=2)

    x_all = np.ascontiguousarray(x0.transpose(2, 0, 1))
    x_all = from_b(run_l0(bslices(x_all), l0w))
    x_all = from_p(run_f1(pslices(x_all), f1w))
    x_all = from_b(run_a1(bslices(x_all), a1w))
    x_all = from_p(run_f1(pslices(x_all), f1w))
    return np.ascontiguousarray(x_all.transpose(1, 2, 0))
